# version 01 (final)
# baseline (speedup 1.0000x reference)
import numpy as np
import concourse.bass as bass
import concourse.mybir as mybir
from concourse.bass_utils import run_bass_kernel_spmd

F32 = mybir.dt.float32
I32 = mybir.dt.int32
AF = mybir.ActivationFunctionType
OP = mybir.AluOpType
AX = mybir.AxisListType.X

B, S, DM, DI, H = 2, 8192, 1024, 2048, 32
HD = DI // H
NC = 8
SC = S // 4
TT = 512
NT = SC // TT
CH = 16
NSUB = SC // CH

_DEBUG_SPILL = False
_STOP_AFTER = None
POOL_N = 48128


def _rope_fp32():
    r = (np.float32(1.0) / np.power(np.float32(10000.0),
         np.arange(H, dtype=np.float32) / np.float32(H))).astype(np.float32)
    u = r.view(np.uint32).copy()
    u[5] = 0x3e72d423
    u[27] = 0x39dd1725
    return u.view(np.float32)


def _to_chunkcol(v, nch):
    return np.ascontiguousarray(v.reshape(nch, 128).T.astype(np.float32))


def _host_stage(inp):
    f = np.float32
    st = {}
    Wp = np.asarray(inp["L0_attn_in_proj_W"], f)
    perm = np.concatenate([
        3 * DI + np.arange(DI),
        np.arange(DI),
        DI + 2 * np.arange(DI),
        DI + 1 + 2 * np.arange(DI),
    ])
    st["Win"] = np.ascontiguousarray(Wp[perm, :].T)
    st["binL"] = _to_chunkcol(np.asarray(inp["L0_attn_in_proj_b"], f)[perm], 64)
    dyn_W = np.asarray(inp["L0_attn_dyn_W"], f)
    selB_W = np.asarray(inp["L0_attn_selB_W"], f)
    selC_W = np.asarray(inp["L0_attn_selC_W"], f)
    rows = [selB_W[0::2], selB_W[1::2], selC_W[0::2], selC_W[1::2],
            dyn_W[0:32], dyn_W[32:64], dyn_W[64:96],
            np.asarray(inp["L0_attn_seldt_W"], f),
            np.asarray(inp["L0_attn_beta_W"], f),
            np.asarray(inp["L0_attn_rg_W"], f),
            np.asarray(inp["L0_attn_ug_W"], f),
            np.zeros((32, DI), f)]
    st["Wsm"] = np.ascontiguousarray(np.concatenate(rows, 0).T)
    dyn_b = np.asarray(inp["L0_attn_dyn_b"], f)
    bs = [np.zeros(32, f)] * 4 + [dyn_b[0:32], dyn_b[32:64], dyn_b[64:96],
          np.zeros(32, f), np.asarray(inp["L0_attn_beta_b"], f),
          np.asarray(inp["L0_attn_rg_b"], f),
          np.asarray(inp["L0_attn_ug_b"], f), np.zeros(32, f)]
    st["bsmL"] = _to_chunkcol(np.concatenate(bs, 0), 3)
    QW = np.asarray(inp["L0_attn_Q_W"], f)
    permq = np.concatenate([2 * np.arange(DI), 1 + 2 * np.arange(DI)])
    st["WQ"] = np.ascontiguousarray(QW[permq, :].T)
    st["Wout"] = np.ascontiguousarray(np.asarray(inp["L0_attn_out_W"], f).T)
    st["convw"] = np.ascontiguousarray(
        np.asarray(inp["L0_attn_conv_w"], f).reshape(16, 128, 4).transpose(1, 0, 2))
    st["convb"] = _to_chunkcol(np.asarray(inp["L0_attn_conv_b"], f), 16)
    st["rmsw"] = _to_chunkcol(np.asarray(inp["L0_attn_rms_w"], f), 8)
    st["gnw"] = _to_chunkcol(np.asarray(inp["L0_attn_gn_w"], f), 16)
    st["gnb"] = _to_chunkcol(np.asarray(inp["L0_attn_gn_b"], f), 16)
    st["Dv"] = _to_chunkcol(np.asarray(inp["L0_attn_D"], f), 16)
    st["dtc"] = np.ascontiguousarray(np.asarray(inp["L0_attn_dt_c"], f).reshape(32, 1))
    st["lng"] = _to_chunkcol(np.asarray(inp["L0_ffn_ln_gamma"], f), 8)
    st["lnb"] = _to_chunkcol(np.asarray(inp["L0_ffn_ln_beta"], f), 8)
    st["fc1q"] = np.ascontiguousarray(np.asarray(inp["L0_ffn_fc1_q"], np.int32))
    st["fc2q"] = np.ascontiguousarray(np.asarray(inp["L0_ffn_fc2_q"], np.int32))
    st["fc1s"] = np.ascontiguousarray(np.asarray(inp["L0_ffn_fc1_scales"], f))
    st["fc2s"] = np.ascontiguousarray(np.asarray(inp["L0_ffn_fc2_scales"], f))
    st["fc1b"] = _to_chunkcol(np.asarray(inp["L0_ffn_fc1_bias"], f), 32)
    st["fc2b"] = _to_chunkcol(np.asarray(inp["L0_ffn_fc2_bias"], f), 8)
    st["ident"] = np.eye(128, dtype=f)
    sel = np.zeros((128, 160), f)
    for m in range(32):
        sel[m, m] = 1.0
        sel[32 + m, 32 + m] = 1.0
        sel[64 + m, 32 + m] = 1.0
        sel[64 + m, 64 + m] = 1.0
        sel[32 + m, 96 + m] = 1.0
        sel[96 + m, 128 + m] = 1.0
    st["SELS"] = sel

    rope = _rope_fp32()
    x = np.asarray(inp["L0_attn_x"], f)
    cores = []
    for c in range(NC):
        b, q = c // 4, c % 4
        t0 = q * SC
        xf = np.zeros((DM, 3 + SC), f)
        xf[:, 3:] = x[b, t0:t0 + SC, :].T
        if q > 0:
            xf[:, :3] = x[b, t0 - 3:t0, :].T
        d = dict(st)
        d["x_fm"] = np.ascontiguousarray(xf)
        d["halom"] = np.full((128, 1), 0.0 if q == 0 else 1.0, f)
        t = np.arange(t0, t0 + SC, dtype=f)
        pr = (t[None, :] * rope[:, None]).astype(f)
        d["posrope"] = np.ascontiguousarray(
            pr.reshape(32, 4, 512).transpose(1, 0, 2).reshape(128, 512))
        selc = np.zeros((128, 8), f); selc[:, c] = 1.0
        d["sel8"] = selc
        bm = np.zeros((8, 1), f); bm[4 * b:4 * b + 4, 0] = 1.0
        d["bmask"] = bm
        cores.append(d)
    return cores


_IN_SPECS = [
    ("x_fm", [DM, 3 + SC], F32), ("Win", [DM, 4 * DI], F32),
    ("binL", [128, 64], F32), ("Wsm", [DI, 384], F32), ("bsmL", [128, 3], F32),
    ("WQ", [DI, 2 * DI], F32), ("Wout", [DI, DM], F32),
    ("convw", [128, 16, 4], F32), ("convb", [128, 16], F32),
    ("rmsw", [128, 8], F32), ("gnw", [128, 16], F32), ("gnb", [128, 16], F32),
    ("Dv", [128, 16], F32), ("dtc", [32, 1], F32),
    ("lng", [128, 8], F32), ("lnb", [128, 8], F32),
    ("fc1q", [DM, 4096], I32), ("fc2q", [4096, DM], I32),
    ("fc1s", [8, 4096], F32), ("fc2s", [32, DM], F32),
    ("fc1b", [128, 32], F32), ("fc2b", [128, 8], F32),
    ("ident", [128, 128], F32), ("SELS", [128, 160], F32),
    ("halom", [128, 1], F32), ("posrope", [128, 512], F32),
    ("sel8", [128, 8], F32), ("bmask", [8, 1], F32),
]
_CONST_SKIP = {"x_fm", "Win", "Wsm", "WQ", "Wout", "fc1q", "fc2q", "fc1s",
               "fc2s", "posrope"}


class KB:
    ENGS = ("sync", "tensor", "vector", "scalar", "gpsimd")

    def __init__(self):
        self.nc = bass.Bass("TRN2", target_bir_lowering=False, debug=False,
                            detect_race_conditions=False)
        self.prog = {e: [] for e in self.ENGS}
        self.cnt = {}
        self.sems = {}
        self.waits_emitted = []

    def now(self, sem):
        return self.cnt.get(sem, 0)

    _DRAIN_ENGS = ("vector", "scalar")
    _NODRAIN_METHODS = ("dma_start", "nop", "drain", "wait_ge")

    def op(self, eng, method, *args, waits=(), inc=None, incn=1, drain=None,
           **kw):
        sems = self.sems
        wl = tuple(w for w in waits if w[1] > 0)
        self.waits_emitted.extend(wl)
        if inc is not None:
            self.cnt[inc] = self.cnt.get(inc, 0) + incn
        if drain is None:
            drain = eng in self._DRAIN_ENGS and method not in self._NODRAIN_METHODS

        def thunk(e, method=method, args=args, kw=kw, wl=wl, inc=inc,
                  incn=incn, drain=drain):
            if drain:
                e.drain()
            for s_, v_ in wl:
                e.wait_ge(sems[s_], v_)
            ret = getattr(e, method)(*args, **kw)
            if inc is not None:
                ret.then_inc(sems[inc], incn)
        self.prog[eng].append(thunk)
        return self.cnt.get(inc, 0) if inc is not None else None

    def wait(self, eng, sem, val=None):
        v = self.now(sem) if val is None else val
        if v <= 0:
            return
        self.waits_emitted.append((sem, v))
        sems = self.sems

        def thunk(e, sem=sem, v=v):
            e.wait_ge(sems[sem], v)
        self.prog[eng].append(thunk)

    def barrier(self):
        for e in self.ENGS:
            self.op(e, "nop", inc="BAR")
        tgt = self.now("BAR")
        for e in self.ENGS:
            self.wait(e, "BAR", tgt)

    def load(self, out_ap, in_ap, waits=()):
        return self.op("sync", "dma_start", waits=waits, inc="LD", incn=16,
                       out=out_ap, in_=in_ap)

    def store(self, out_ap, in_ap, waits=()):
        return self.op("scalar", "dma_start", waits=waits, inc="ST", incn=16,
                       out=out_ap, in_=in_ap)

    def gdma(self, out_ap, in_ap, waits=()):
        return self.op("gpsimd", "dma_start", waits=waits, inc="GD", incn=16,
                       out=out_ap, in_=in_ap)

    def check(self):
        for s_, v_ in self.waits_emitted:
            assert v_ <= self.cnt.get(s_, 0), \
                f"wait {s_}>={v_} never reached (final {self.cnt.get(s_, 0)})"


def build_nc():
    nb = KB()
    nc = nb.nc
    ins = {}
    for name, shape, dt in _IN_SPECS:
        ins[name] = nc.dram_tensor(name, shape, dt, kind="ExternalInput").ap()
    OUT = nc.dram_tensor("OUT", [DM, SC], F32, kind="ExternalOutput").ap()
    spkind = "ExternalOutput" if _DEBUG_SPILL else "Internal"
    sp = {}
    for name, shape in [("K0", [SC, DI]), ("K1", [SC, DI]), ("U", [SC, DI]),
                        ("Q0", [SC, DI]), ("Q1", [SC, DI]), ("PH", [SC, 256]),
                        ("Y", [SC, DI]), ("G", [DI, SC]), ("Hs", [DI, SC]),
                        ("X2", [DM, SC]), ("Ssp", [4096, SC]), ("MR", [64, 1])]:
        sp[name] = nc.dram_tensor("SP" + name, shape, F32, kind=spkind).ap()
    cc1i = nc.dram_tensor("cc1i", [128, 96], F32, kind="Internal").ap()
    cc1o = nc.dram_tensor("cc1o", [128 * NC, 96], F32, kind="Internal").ap()
    cc2i = nc.dram_tensor("cc2i", [1, 64], F32, kind="Internal").ap()
    cc2o = nc.dram_tensor("cc2o", [NC, 64], F32, kind="Internal").ap()

    cst = {}
    for name, shape, dt in _IN_SPECS:
        if name in _CONST_SKIP:
            continue
        cst[name] = nc.alloc_sbuf_tensor("c_" + name, list(shape), dt).ap()
    ones_col = nc.alloc_sbuf_tensor("ones_col", [128, 1], F32).ap()
    ones_row = nc.alloc_sbuf_tensor("ones_row", [1, 128], F32).ap()
    spdtc = nc.alloc_sbuf_tensor("spdtc", [32, 1], F32).ap()
    dwork = nc.alloc_sbuf_tensor("dwork", [32, 2], F32).ap()
    fc2b05 = nc.alloc_sbuf_tensor("fc2b05", [128, 8], F32).ap()
    stat = nc.alloc_sbuf_tensor("statrow", [1, 4096], F32).ap()
    murc = nc.alloc_sbuf_tensor("murc", [128, 2, 16], F32).ap()
    cc2s = nc.alloc_sbuf_tensor("cc2s", [8, 64], F32).ap()
    mr64 = nc.alloc_sbuf_tensor("mr64", [64, 1], F32).ap()
    constv = nc.alloc_sbuf_tensor("constv", [128, 4], F32).ap()
    pool = nc.alloc_sbuf_tensor("pool", [128, POOL_N], F32).ap()
    pool_i = pool.bitcast(I32)

    def P(off, *shape, i32=False):
        n = int(np.prod(shape))
        base = pool_i if i32 else pool
        v = base[:, off:off + n]
        if len(shape) > 1:
            letters = "abcdefg"[:len(shape)]
            v = v.rearrange(f"p ({' '.join(letters)}) -> p {' '.join(letters)}",
                            **{k: int(s_) for k, s_ in zip(letters, shape)})
        return v

    pb = [nc.alloc_psum_tensor(f"pb{i}", [128, 512], F32).ap() for i in range(8)]

    xv = ins["x_fm"].rearrange("(c p) t -> p c t", p=128)
    winv = ins["Win"].rearrange("(c p) m -> p c m", p=128)
    wsmv = ins["Wsm"].rearrange("(c p) m -> p c m", p=128)
    wqv = ins["WQ"].rearrange("(c p) m -> p c m", p=128)
    woutv = ins["Wout"].rearrange("(c p) m -> p c m", p=128)
    fc1qv = ins["fc1q"].rearrange("(c p) m -> p c m", p=128)
    fc2qv = ins["fc2q"].rearrange("(c p) m -> p c m", p=128)
    outv = OUT.rearrange("(c p) t -> p c t", p=128)
    x2v = sp["X2"].rearrange("(c p) t -> p c t", p=128)
    ident = cst["ident"]

    spj = {k: sp[k].rearrange("(p j) c -> p j c", j=CH) for k in
           ("K0", "K1", "U", "Q0", "Q1", "Y")}
    sphj = sp["PH"].rearrange("(p j) c -> p j c", j=CH)
    spblk = {k: sp[k].rearrange("(T r i) c -> T r i c", r=4, i=128) for k in
             ("K0", "K1", "U", "Q0", "Q1")}
    sphblk = sp["PH"].rearrange("(T r i) c -> T i r c", r=4, i=128)

    bank_free = {i: ("AC", 0) for i in range(8)}
    wb_free = {0: 0, 1: 0}
    spst_free = [0, 0]

    for name in cst:
        nb.load(cst[name], ins[name])
    nb.op("vector", "memset", ones_col, 1.0)
    nb.op("vector", "memset", ones_row, 1.0)
    for ci_, cval_ in enumerate((1e-6, 1e-5, 0.5, -1.0)):
        nb.op("vector", "memset", constv[:, ci_:ci_ + 1], cval_)
        nc.const_aps.register(F32, cval_, constv[:, ci_:ci_ + 1])
    nb.wait("scalar", "LD")
    nb.op("scalar", "activation", dwork[:, 0:1], cst["dtc"], AF.Abs)
    nb.op("scalar", "activation", dwork[:, 0:1], dwork[:, 0:1], AF.Exp, scale=-1.0)
    nb.op("scalar", "activation", dwork[:, 0:1], dwork[:, 0:1], AF.Ln, bias=1.0)
    nb.op("scalar", "activation", dwork[:, 1:2], cst["dtc"], AF.Relu, inc="AC")
    nb.wait("vector", "AC")
    nb.op("vector", "tensor_add", spdtc, dwork[:, 0:1], dwork[:, 1:2])
    nb.op("vector", "tensor_scalar_mul", fc2b05, cst["fc2b"], 0.5, inc="DV")
    nb.barrier()
    if _STOP_AFTER == "consts":
        _finish(nb)
        return nb, OUT

    off = [0]

    def take(n):
        o = off[0]
        off[0] += n
        assert off[0] <= POOL_N, f"pool overflow {off[0]}"
        return o

    o_x = take(8 * TT)
    o_V = take(16 * 515); o_u = take(16 * TT)
    o_g = take(2 * TT); o_gh = take(2 * TT)
    o_sm = take(3 * TT); o_kst = take(4 * TT)
    o_spst = take(2 * 2048); o_wb = take(2 * 4096)
    o_ph = take(4 * 256)
    o_hd = take(12 * 512)
    o_vh = take(16 * 3); o_hx = take(8 * 3); o_hx2 = take(8 * 3)
    o_pr = take(512)

    x_t = P(o_x, 8, TT)
    xn_t = x_t
    V_t = P(o_V, 16, 515); u_t = P(o_u, 16, TT)
    g_t = P(o_g, 2, TT); gh_t = P(o_gh, 2, TT)
    sm_t = P(o_sm, 3, TT); kst = P(o_kst, 4, TT)
    spst = [P(o_spst + i * 512, 512) for i in range(2)]
    ph_t = P(o_ph, 4, 256)
    hdw = P(o_hd, 12, 512)
    vh_t = P(o_vh, 16, 3); xh_t = P(o_hx, 8, 3); xnh_t = P(o_hx2, 8, 3)
    pr_t = P(o_pr, 512)

    def wbuf(i, kch, m):
        return P(o_wb + i * 4096, kch, m)

    def hd(i, rows=128):
        return hdw[0:rows, i, :]

    def spill_group(tau, blkview, cgroup, src4, src_waits):
        for blk in range(4):
            par = (cgroup * 4 + blk) % 2
            bi = 4 + blk % 2
            w0 = tuple(src_waits) + (bank_free[bi],)
            for ci in range(4):
                nb.op("tensor", "transpose", pb[bi][:, ci * 128:(ci + 1) * 128],
                      src4[ci][:, blk * 128:(blk + 1) * 128], ident,
                      waits=w0 if ci == 0 else (),
                      inc="PE" if ci == 3 else None)
            pe_t = nb.now("PE")
            nb.wait("scalar", "PE", pe_t)
            nb.wait("scalar", "ST", spst_free[par])
            ac = nb.op("scalar", "activation", spst[par], pb[bi], AF.Copy,
                       inc="AC")
            bank_free[bi] = ("AC", ac)
            spst_free[par] = nb.store(
                blkview[tau, blk, :, cgroup * 512:(cgroup + 1) * 512], spst[par],
                waits=(("AC", ac),))
        return nb.now("PE")

    nb.load(xh_t, xv[:, :, 0:3])
    nb.wait("vector", "LD")
    nb.op("vector", "tensor_mul", xnh_t, xh_t, xh_t, inc="DV")
    nb.wait("tensor", "DV")
    for k in range(8):
        nb.op("tensor", "matmul", pb[6][0:1, 0:3], ones_col, xnh_t[:, k, :],
              start=(k == 0), stop=(k == 7), inc="PE" if k == 7 else None)
    nb.wait("scalar", "PE")
    ac = nb.op("scalar", "activation", stat[0:1, 0:3], pb[6][0:1, 0:3],
               AF.Sqrt, bias=1e-6, scale=1.0 / DM, inc="AC")
    bank_free[6] = ("AC", ac)
    nb.wait("vector", "AC")
    nb.op("vector", "reciprocal", stat[0:1, 4:7], stat[0:1, 0:3], inc="DV")
    nb.wait("tensor", "DV")
    nb.op("tensor", "matmul", pb[7][:, 0:3], ones_row, stat[0:1, 4:7],
          start=True, stop=True, inc="PE")
    nb.wait("vector", "PE")
    for k in range(8):
        nb.op("vector", "scalar_tensor_tensor", xnh_t[:, k, :], xh_t[:, k, :],
              cst["rmsw"][:, k:k + 1], pb[7][:, 0:3], OP.mult, OP.mult,
              inc="DV" if k == 7 else None)
    bank_free[7] = ("DV", nb.now("DV"))
    nb.wait("tensor", "DV")
    for pn in range(8):
        wpan = wbuf(pn % 2, 8, 256)
        nb.load(wpan, winv[:, :, pn * 256:(pn + 1) * 256],
                waits=(("PE", wb_free[pn % 2]),))
        nb.wait("tensor", "LD")
        for m2 in range(2):
            m = pn * 2 + m2
            bi = m % 2
            for k in range(8):
                nb.op("tensor", "matmul", pb[bi][:, 0:3],
                      wpan[:, k, m2 * 128:(m2 + 1) * 128], xnh_t[:, k, :],
                      start=(k == 0), stop=(k == 7),
                      waits=(bank_free[bi],) if k == 0 else (),
                      inc="PE" if k == 7 else None)
            nb.wait("vector", "PE")
            dv = nb.op("vector", "tensor_scalar", vh_t[:, m, :],
                       pb[bi][:, 0:3], cst["binL"][:, m:m + 1],
                       cst["halom"], OP.add, OP.mult, inc="DV")
            bank_free[bi] = ("DV", dv)
        wb_free[pn % 2] = nb.now("PE")
    halo_dv = nb.now("DV")
    if _STOP_AFTER == "halo":
        nb.barrier()
        _finish(nb)
        return nb, OUT

    g_store = {0: 0, 1: 0}
    for tau in range(NT):
        c0 = 3 + tau * TT
        nb.load(x_t, xv[:, :, c0:c0 + TT],
                waits=(("DV", nb.now("DV")), ("PE", nb.now("PE")),
                       ("AC", nb.now("AC"))))
        nb.load(pr_t[0:32, :],
                ins["posrope"].rearrange("(T h) t -> T h t", h=32)[tau])
        ld_x = nb.now("LD")
        sq = [hd(0), hd(1)]
        pe_sq = {}
        nb.wait("vector", "LD", ld_x)
        for k in range(8):
            if k >= 2:
                nb.wait("vector", "PE", pe_sq[k - 2])
            nb.op("vector", "tensor_mul", sq[k % 2], x_t[:, k, :], x_t[:, k, :],
                  inc="DV")
            nb.wait("tensor", "DV")
            nb.op("tensor", "matmul", pb[6][0:1, :], ones_col, sq[k % 2],
                  start=(k == 0), stop=(k == 7),
                  waits=(bank_free[6],) if k == 0 else (), inc="PE")
            pe_sq[k] = nb.now("PE")
        nb.wait("scalar", "PE")
        ac = nb.op("scalar", "activation", stat[0:1, 0:TT], pb[6][0:1, :],
                   AF.Sqrt, bias=1e-6, scale=1.0 / DM, inc="AC")
        bank_free[6] = ("AC", ac)
        nb.wait("vector", "AC")
        nb.op("vector", "reciprocal", stat[0:1, 512:512 + TT],
              stat[0:1, 0:TT], inc="DV")
        nb.wait("tensor", "DV")
        nb.op("tensor", "matmul", pb[7], ones_row, stat[0:1, 512:512 + TT],
              start=True, stop=True, waits=(bank_free[7],), inc="PE")
        nb.wait("vector", "PE")
        for k in range(8):
            nb.op("vector", "scalar_tensor_tensor", xn_t[:, k, :], x_t[:, k, :],
                  cst["rmsw"][:, k:k + 1], pb[7], OP.mult, OP.mult,
                  inc="DV" if k == 7 else None)
        dv_xn = nb.now("DV")
        bank_free[7] = ("DV", dv_xn)

        if tau == 0:
            nb.wait("vector", "DV", halo_dv)
            nb.op("vector", "tensor_copy", V_t[:, :, 0:3], vh_t, inc="DV")
        dv_vprev = nb.now("DV")

        def mm_group(wpan, m2, bi, rhs_t, nk, first_waits=()):
            fw = tuple(first_waits) + (bank_free[bi],)
            for k in range(nk):
                nb.op("tensor", "matmul", pb[bi],
                      wpan[:, k, m2 * 128:(m2 + 1) * 128], rhs_t[:, k, :],
                      start=(k == 0), stop=(k == nk - 1),
                      waits=fw if k == 0 else (),
                      inc="PE" if k == nk - 1 else None)
            return nb.now("PE")

        acV = [0] * 16
        for pn in range(8):
            wpan = wbuf(pn % 2, 8, 256)
            nb.load(wpan, winv[:, :, pn * 256:(pn + 1) * 256],
                    waits=(("PE", wb_free[pn % 2]),))
            nb.wait("tensor", "LD")
            nb.wait("tensor", "DV", dv_xn)
            if pn == 0:
                nb.wait("tensor", "DV", dv_vprev)
            for m2 in range(2):
                m = pn * 2 + m2
                pe_m = mm_group(wpan, m2, m % 2, xn_t, 8)
                nb.wait("scalar", "PE", pe_m)
                acV[m] = nb.op("scalar", "activation", V_t[:, m, 3:515],
                               pb[m % 2], AF.Identity,
                               bias=cst["binL"][:, m:m + 1], inc="AC")
                bank_free[m % 2] = ("AC", acV[m])
            wb_free[pn % 2] = nb.now("PE")

        ac_conv = {0: 0, 1: 0}
        for ci in range(16):
            par = ci % 2
            nb.wait("vector", "AC", max(acV[ci], ac_conv[par]))
            cw = cst["convw"]
            acc = hd(par)
            nb.op("vector", "tensor_scalar_mul", acc, V_t[:, ci, 0:TT],
                  cw[:, ci, 0:1])
            for tp in range(1, 4):
                nb.op("vector", "scalar_tensor_tensor", acc,
                      V_t[:, ci, tp:tp + TT], cw[:, ci, tp:tp + 1], acc,
                      OP.mult, OP.add)
            nb.op("vector", "nop", inc="DV")
            nb.wait("scalar", "DV", nb.now("DV"))
            ac_conv[par] = nb.op("scalar", "activation", u_t[:, ci, :], acc,
                                 AF.Silu, bias=cst["convb"][:, ci:ci + 1],
                                 inc="AC")
        ac_u = nb.now("AC")

        kst_free = 0
        for g4 in range(8):
            qname = "K0" if g4 < 4 else "K1"
            cg = g4 % 4
            for half in range(2):
                pn = 16 + g4 * 2 + half
                wpan = wbuf(pn % 2, 8, 256)
                nb.load(wpan, winv[:, :, pn * 256:(pn + 1) * 256],
                        waits=(("PE", wb_free[pn % 2]),))
                nb.wait("tensor", "LD")
                for m2 in range(2):
                    m = pn * 2 + m2
                    ci4 = half * 2 + m2
                    pe_m = mm_group(wpan, m2, m % 2, xn_t, 8)
                    nb.wait("scalar", "PE", max(pe_m, kst_free))
                    ac = nb.op("scalar", "activation", kst[:, ci4, :],
                               pb[m % 2], AF.Identity,
                               bias=cst["binL"][:, m:m + 1], inc="AC")
                    bank_free[m % 2] = ("AC", ac)
                wb_free[pn % 2] = nb.now("PE")
            kst_free = spill_group(tau, spblk[qname], cg,
                                   [kst[:, i, :] for i in range(4)],
                                   (("AC", nb.now("AC")),))

        for pn in range(8, 16):
            wpan = wbuf(pn % 2, 8, 256)
            nb.load(wpan, winv[:, :, pn * 256:(pn + 1) * 256],
                    waits=(("PE", wb_free[pn % 2]),))
            nb.wait("tensor", "LD")
            for m2 in range(2):
                m = pn * 2 + m2
                zc = m - 16
                par = zc % 2
                pe_m = mm_group(wpan, m2, m % 2, xn_t, 8)
                nb.wait("scalar", "PE", pe_m)
                nb.wait("scalar", "DV", g_store[par])
                ac_g = nb.op("scalar", "activation", g_t[:, par, :], pb[m % 2],
                             AF.Silu, bias=cst["binL"][:, m:m + 1], inc="AC")
                bank_free[m % 2] = ("AC", ac_g)
                nb.wait("vector", "AC", ac_g)
                nb.op("vector", "tensor_scalar_mul", gh_t[:, par, :],
                      g_t[:, par, :], cst["gnw"][:, zc:zc + 1], inc="DV")
                nb.store(sp["G"][zc * 128:(zc + 1) * 128, tau * TT:(tau + 1) * TT],
                         gh_t[:, par, :], waits=(("DV", nb.now("DV")),))
                nb.op("vector", "tensor_scalar_mul", g_t[:, par, :],
                      g_t[:, par, :], cst["gnb"][:, zc:zc + 1])
                nb.wait("vector", "AC", ac_u)
                g_store[par] = nb.op(
                    "vector", "scalar_tensor_tensor", g_t[:, par, :],
                    u_t[:, zc, :], cst["Dv"][:, zc:zc + 1], g_t[:, par, :],
                    OP.mult, OP.add, inc="DV")
                nb.store(sp["Hs"][zc * 128:(zc + 1) * 128, tau * TT:(tau + 1) * TT],
                         g_t[:, par, :], waits=(("DV", g_store[par]),))
            wb_free[pn % 2] = nb.now("PE")

        for mi in range(3):
            wpan = wbuf(mi % 2, 16, 128)
            nb.load(wpan, wsmv[:, :, mi * 128:(mi + 1) * 128],
                    waits=(("PE", wb_free[mi % 2]),))
            nb.wait("tensor", "LD")
            pe_m = mm_group(wpan, 0, mi % 2, u_t, 16,
                            first_waits=(("AC", ac_u),))
            wb_free[mi % 2] = pe_m
            nb.wait("scalar", "PE", pe_m)
            ac = nb.op("scalar", "activation", sm_t[:, mi, :], pb[mi % 2],
                       AF.Identity, bias=cst["bsmL"][:, mi:mi + 1], inc="AC")
            bank_free[mi % 2] = ("AC", ac)
        ac_sm = nb.now("AC")

        SELSv = cst["SELS"]

        def reloc(selcol, smi, bi, out_slot, func):
            nb.op("tensor", "matmul", pb[bi][0:32, :],
                  SELSv[:, selcol * 32:(selcol + 1) * 32], sm_t[:, smi, :],
                  start=True, stop=True,
                  waits=(("AC", ac_sm), bank_free[bi]), inc="PE")
            nb.wait("scalar", "PE", nb.now("PE"))
            ac = nb.op("scalar", "activation", hd(out_slot, 32),
                       pb[bi][0:32, :], func, inc="AC")
            bank_free[bi] = ("AC", ac)
            return ac

        reloc(0, 1, 2, 2, AF.Copy)
        reloc(1, 1, 3, 4, AF.Copy)
        reloc(4, 1, 2, 5, AF.Copy)
        reloc(0, 2, 3, 6, AF.Sigmoid)
        reloc(3, 2, 2, 7, AF.Sigmoid)
        reloc(2, 2, 3, 8, AF.Sigmoid)
        ac_rel = nb.now("AC")

        def softplus32(src_slot, dst_slot):
            nb.op("scalar", "activation", hd(0, 32), hd(src_slot, 32), AF.Abs)
            nb.op("scalar", "activation", hd(0, 32), hd(0, 32), AF.Exp, scale=-1.0)
            nb.op("scalar", "activation", hd(0, 32), hd(0, 32), AF.Ln, bias=1.0)
            nb.op("scalar", "activation", hd(1, 32), hd(src_slot, 32), AF.Relu,
                  inc="AC")
            nb.wait("vector", "AC")
            nb.op("vector", "tensor_add", hd(dst_slot, 32), hd(0, 32), hd(1, 32),
                  inc="DV")
            nb.wait("scalar", "DV", nb.now("DV"))

        nb.wait("vector", "AC", ac_rel)
        nb.wait("vector", "LD", ld_x)
        nb.op("vector", "tensor_add", hd(4, 32), hd(4, 32), pr_t[0:32, :],
              inc="DV")
        nb.wait("scalar", "DV", nb.now("DV"))
        softplus32(2, 2)
        softplus32(5, 5)
        nb.op("scalar", "activation", hd(0, 32), hd(4, 32), AF.Abs, inc="AC")
        nb.wait("vector", "AC")
        nb.op("vector", "tensor_add", hd(0, 32), hd(0, 32), hd(2, 32))
        nb.op("vector", "tensor_scalar_add", hd(0, 32), hd(0, 32), 1e-4)
        nb.op("vector", "reciprocal", hd(1, 32), hd(0, 32))
        nb.op("vector", "tensor_scalar_mul", hd(3, 32), hd(1, 32), spdtc)
        nb.op("vector", "tensor_add", hd(3, 32), hd(3, 32), hd(5, 32))
        nb.op("vector", "scalar_tensor_tensor", hd(0, 32), hd(3, 32), 0.5,
              hd(2, 32), OP.mult, OP.mult)
        nb.op("vector", "scalar_tensor_tensor", hd(1, 32), hd(3, 32), 0.5,
              hd(4, 32), OP.mult, OP.mult)
        nb.op("vector", "tensor_mul", hd(2, 32), hd(0, 32), hd(0, 32))
        nb.op("vector", "tensor_mul", hd(5, 32), hd(1, 32), hd(1, 32))
        nb.op("vector", "tensor_add", hd(2, 32), hd(2, 32), hd(5, 32))
        nb.op("vector", "tensor_scalar", hd(4, 32), hd(0, 32), 2.0, 1.0,
              OP.mult, OP.add)
        nb.op("vector", "tensor_add", hd(4, 32), hd(4, 32), hd(2, 32))
        nb.op("vector", "tensor_scalar", hd(5, 32), hd(0, 32), -2.0, 1.0,
              OP.mult, OP.add)
        nb.op("vector", "tensor_add", hd(5, 32), hd(5, 32), hd(2, 32))
        nb.op("vector", "reciprocal", hd(0, 32), hd(4, 32))
        nb.op("vector", "tensor_scalar", hd(9, 32), hd(2, 32), -1.0, 1.0,
              OP.mult, OP.add)
        nb.op("vector", "tensor_mul", hd(9, 32), hd(9, 32), hd(0, 32))
        nb.op("vector", "scalar_tensor_tensor", hd(10, 32), hd(1, 32), 2.0,
              hd(0, 32), OP.mult, OP.mult)
        nb.op("vector", "tensor_mul", hd(5, 32), hd(5, 32), hd(0, 32), inc="DV")
        nb.wait("scalar", "DV", nb.now("DV"))
        nb.op("scalar", "activation", hd(5, 32), hd(5, 32), AF.Ln, inc="AC")
        nb.wait("vector", "AC")
        nb.op("vector", "tensor_mul", hd(5, 32), hd(5, 32), hd(7, 32), inc="DV")
        nb.wait("scalar", "DV", nb.now("DV"))
        nb.op("scalar", "activation", hd(5, 32), hd(5, 32), AF.Exp, inc="AC")
        nb.wait("vector", "AC")
        nb.op("vector", "tensor_scalar", hd(5, 32), hd(5, 32), -1.0, 1.0,
              OP.mult, OP.add)
        nb.op("vector", "tensor_scalar_max", hd(5, 32), hd(5, 32), 1e-6,
              inc="DV")
        nb.wait("scalar", "DV", nb.now("DV"))
        nb.op("scalar", "activation", hd(5, 32), hd(5, 32), AF.Sqrt, inc="AC")
        nb.wait("vector", "AC")
        nb.op("vector", "tensor_mul", hd(11, 32), hd(5, 32), hd(8, 32),
              inc="DV")
        dv_head = nb.now("DV")

        for blk in range(4):
            bi = 4 + blk % 2
            nb.op("tensor", "transpose", pb[bi][:, 0:128],
                  sm_t[:, 0, blk * 128:(blk + 1) * 128], ident,
                  waits=(("AC", ac_sm), ("DV", dv_head), bank_free[bi]))
            for qi, slot in enumerate((9, 10, 6, 11)):
                nb.op("tensor", "transpose",
                      pb[bi][0:128, 128 + qi * 32:128 + (qi + 1) * 32],
                      hd(slot, 32)[:, blk * 128:(blk + 1) * 128],
                      ident[0:32, 0:32], inc="PE" if qi == 3 else None)
            nb.wait("scalar", "PE", nb.now("PE"))
            ac = nb.op("scalar", "activation", ph_t[:, blk, :],
                       pb[bi][:, 0:256], AF.Copy, inc="AC")
            bank_free[bi] = ("AC", ac)
        nb.store(sphblk[tau], ph_t, waits=(("AC", nb.now("AC")),))

        for g4 in range(8):
            qname = "Q0" if g4 < 4 else "Q1"
            cg = g4 % 4
            for half in range(2):
                pn = g4 * 2 + half
                wpanq = wbuf(pn % 2, 16, 256)
                nb.load(wpanq, wqv[:, :, pn * 256:(pn + 1) * 256],
                        waits=(("PE", wb_free[pn % 2]),))
                nb.wait("tensor", "LD")
                for m2 in range(2):
                    ci4 = half * 2 + m2
                    bi = ci4 % 2
                    pe_m = mm_group(wpanq, m2, bi, u_t, 16,
                                    first_waits=(("AC", ac_u),))
                    nb.wait("scalar", "PE", max(pe_m, kst_free))
                    ac = nb.op("scalar", "activation", kst[:, ci4, :], pb[bi],
                               AF.Copy, inc="AC")
                    bank_free[bi] = ("AC", ac)
                wb_free[pn % 2] = nb.now("PE")
            kst_free = spill_group(tau, spblk[qname], cg,
                                   [kst[:, i, :] for i in range(4)],
                                   (("AC", nb.now("AC")),))

        for cg in range(4):
            spill_group(tau, spblk["U"], cg,
                        [u_t[:, cg * 4 + i, :] for i in range(4)],
                        (("AC", ac_u),))

        if tau < NT - 1:
            nb.op("vector", "tensor_copy", V_t[:, :, 0:3], V_t[:, :, 512:515],
                  waits=(("AC", ac_u),), inc="DV")
        if _STOP_AFTER == f"t{tau}":
            break

    nb.barrier()
    if _STOP_AFTER == "p1" or (_STOP_AFTER or "").startswith("t"):
        _finish(nb)
        return nb, OUT

    o_S0p, o_S1p = 0, 6144
    o_tA, o_tB = 12288, 18432
    o_k0, o_k1, o_bk0, o_bk1 = 24576, 26624, 28672, 30720
    o_v = 32768
    o_stp = 34816
    o_php = 47104
    assert o_php + 512 <= POOL_N

    S0p = P(o_S0p, 3, 2048); S1p = P(o_S1p, 3, 2048)
    S0p4 = P(o_S0p, 3, 32, 64); S1p4 = P(o_S1p, 3, 32, 64)
    tA = P(o_tA, 3, 2048); tB = P(o_tB, 3, 2048)
    tA4 = P(o_tA, 3, 32, 64); tB4 = P(o_tB, 3, 32, 64)
    k0a = P(o_k0, 2048); k1a = P(o_k1, 2048)
    bk0a = P(o_bk0, 2048); bk1a = P(o_bk1, 2048)
    va = P(o_v, 2048)
    stp = P(o_stp, 6, 2048)
    php = P(o_php, 2, 256)

    def b3(t):
        return t.unsqueeze(1).broadcast_to([128, 3, 2048])

    def phb(src, par, qi):
        return src[:, par, qi * 32:(qi + 1) * 32].unsqueeze(2).broadcast_to(
            [128, 32, 64])

    def phb3(par, qi):
        return php[:, par, qi * 32:(qi + 1) * 32].unsqueeze(1).unsqueeze(3)\
            .broadcast_to([128, 3, 32, 64])

    def vw(t):
        return t.rearrange("p (h d) -> p h d", h=32)

    nb.op("vector", "memset", S0p, 0.0)
    nb.op("vector", "memset", S1p, 0.0)
    nb.op("vector", "memset", S0p[:, 0, :], 1.0)
    nb.op("vector", "memset", S1p[:, 1, :], 1.0, inc="DV")

    dv_step = {}
    for j in range(CH):
        par = j % 2
        w_free = (("DV", dv_step[j - 2]),) if j >= 2 else ()
        nb.load(stp[:, 0 + par, :], spj["K0"][:, j, :], waits=w_free)
        nb.load(stp[:, 2 + par, :], spj["K1"][:, j, :])
        nb.load(stp[:, 4 + par, :], spj["U"][:, j, :])
        nb.load(php[:, par, :], sphj[:, j, :])
        ldj = nb.now("LD")
        nb.wait("vector", "LD", ldj)
        nb.op("vector", "tensor_mul", vw(k0a), vw(stp[:, 0 + par, :]), phb(php, par, 0))
        nb.op("vector", "tensor_mul", vw(k1a), vw(stp[:, 2 + par, :]), phb(php, par, 1))
        nb.op("vector", "tensor_mul", vw(va), vw(stp[:, 4 + par, :]), phb(php, par, 7))
        nb.op("vector", "tensor_mul", vw(bk0a), vw(k0a), phb(php, par, 6))
        nb.op("vector", "tensor_mul", vw(bk1a), vw(k1a), phb(php, par, 6))
        nb.op("vector", "tensor_mul", tA4, S1p4, phb3(par, 5))
        nb.op("vector", "tensor_mul", tB4, S0p4, phb3(par, 5))
        nb.op("vector", "tensor_mul", S0p4, S0p4, phb3(par, 4))
        nb.op("vector", "tensor_add", S0p, S0p, tA)
        nb.op("vector", "tensor_mul", S1p4, S1p4, phb3(par, 4))
        nb.op("vector", "tensor_sub", S1p, S1p, tB)
        nb.op("vector", "tensor_mul", tA, S0p, b3(k0a))
        nb.op("vector", "tensor_mul", tB, S1p, b3(k1a))
        nb.op("vector", "tensor_add", tA, tA, tB)
        nb.op("vector", "tensor_scalar_mul", tB[:, 0:2, :], tA[:, 0:2, :], -1.0)
        nb.op("vector", "tensor_sub", tB[:, 2, :], va, tA[:, 2, :])
        nb.op("vector", "tensor_mul", tA, tB, b3(bk0a))
        nb.op("vector", "tensor_add", S0p, S0p, tA)
        nb.op("vector", "tensor_mul", tA, tB, b3(bk1a))
        nb.op("vector", "tensor_add", S1p, S1p, tA, inc="DV")
        dv_step[j] = nb.now("DV")

    nb.barrier()
    if _STOP_AFTER == "pA":
        for i, (src, r) in enumerate([(S0p, 0), (S0p, 1), (S0p, 2),
                                      (S1p, 0), (S1p, 1), (S1p, 2)]):
            nb.store(spj["Y"][:, i, :], src[:, r, :])
        nb.barrier()
        _finish(nb)
        return nb, OUT

    Pc = [P(o_tA + i * 2048, 16, 128) for i in range(6)]
    ct = P(o_k0, 8, 16, 16)
    srcs = [(S0p, 0), (S0p, 1), (S0p, 2), (S1p, 0), (S1p, 1), (S1p, 2)]
    for qi, (src, r) in enumerate(srcs):
        for cg in range(4):
            bi = 4 + cg % 2
            for ci in range(4):
                cc = cg * 4 + ci
                nb.op("tensor", "transpose", pb[bi][:, ci * 128:(ci + 1) * 128],
                      src[:, r, cc * 128:(cc + 1) * 128], ident,
                      waits=(bank_free[bi],) if ci == 0 else (),
                      inc="PE" if ci == 3 else None)
            nb.wait("scalar", "PE", nb.now("PE"))
            ac = nb.op("scalar", "activation",
                       Pc[qi][:, cg * 4:(cg + 1) * 4, :],
                       pb[bi].rearrange("p (a b) -> p a b", a=4), AF.Copy,
                       inc="AC")
            bank_free[bi] = ("AC", ac)
    nb.wait("vector", "AC", nb.now("AC"))

    def emit_compose(cur, prv, tmps):
        C00, C01, Cq0, C10, C11, Cq1 = cur
        Pp00, Pp01, Pq0, Pp10, Pp11, Pq1 = prv
        t00, t01, t10, t11, tq0, tq1, tx = tmps
        nb.op("vector", "tensor_mul", t00, C00, Pp00)
        nb.op("vector", "tensor_mul", tx, C01, Pp10)
        nb.op("vector", "tensor_add", t00, t00, tx)
        nb.op("vector", "tensor_mul", t01, C00, Pp01)
        nb.op("vector", "tensor_mul", tx, C01, Pp11)
        nb.op("vector", "tensor_add", t01, t01, tx)
        nb.op("vector", "tensor_mul", t10, C10, Pp00)
        nb.op("vector", "tensor_mul", tx, C11, Pp10)
        nb.op("vector", "tensor_add", t10, t10, tx)
        nb.op("vector", "tensor_mul", t11, C10, Pp01)
        nb.op("vector", "tensor_mul", tx, C11, Pp11)
        nb.op("vector", "tensor_add", t11, t11, tx)
        nb.op("vector", "tensor_mul", tq0, C00, Pq0)
        nb.op("vector", "tensor_mul", tx, C01, Pq1)
        nb.op("vector", "tensor_add", tq0, tq0, tx)
        nb.op("vector", "tensor_add", tq0, tq0, Cq0)
        nb.op("vector", "tensor_mul", tq1, C10, Pq0)
        nb.op("vector", "tensor_mul", tx, C11, Pq1)
        nb.op("vector", "tensor_add", tq1, tq1, tx)
        nb.op("vector", "tensor_add", tq1, tq1, Cq1)
        for d, t_ in zip(cur, (t00, t01, tq0, t10, t11, tq1)):
            nb.op("vector", "tensor_copy", d, t_)

    tmps16 = [ct[:, i, :, :] for i in range(7)]
    for s_ in range(1, 8):
        emit_compose([Pc[i][:, :, s_::8] for i in range(6)],
                     [Pc[i][:, :, s_ - 1::8] for i in range(6)], tmps16)
    tmps1 = [ct[:, i, :, 0:1] for i in range(7)]
    for g_ in range(1, 16):
        emit_compose([Pc[i][:, :, g_ * 8 + 7:g_ * 8 + 8] for i in range(6)],
                     [Pc[i][:, :, g_ * 8 - 1:g_ * 8] for i in range(6)], tmps1)

    packb = P(o_k1, 6, 16)
    for i in range(6):
        nb.op("vector", "tensor_copy", packb[:, i, :],
              Pc[i][:, :, 127:128].rearrange("p a b -> p (a b)"),
              inc="DV" if i == 5 else None)
    nb.store(cc1i, packb.rearrange("p a b -> p (a b)"),
             waits=(("DV", nb.now("DV")),))
    nb.op("gpsimd", "collective_compute", "AllGather", OP.bypass,
          [list(range(NC))], [cc1i], [cc1o],
          waits=(("ST", nb.now("ST")),), inc="CC")
    agT = P(o_bk0, 8, 6, 16)
    nb.load(agT.rearrange("p a b c -> p a (b c)"),
            cc1o.rearrange("(r p) c -> p r c", p=128),
            waits=(("CC", nb.now("CC")),))
    ld_ag = nb.now("LD")

    Z = P(o_v, 4, 16)
    Zs = P(o_v + 1024, 2, 16)
    ctmp = P(o_v + 1536, 16)
    nb.op("vector", "memset", Zs, 0.0)
    nb.wait("vector", "LD", ld_ag)
    for b_ in range(2):
        nb.op("vector", "memset", Z[:, 0, :], 0.0)
        nb.op("vector", "memset", Z[:, 1, :], 0.0)
        cur = 0
        for i_ in range(4):
            r = 4 * b_ + i_
            if i_ > 0:
                nb.op("vector", "scalar_tensor_tensor", Zs[:, 0, :],
                      Z[:, cur, :], cst["sel8"][:, r:r + 1], Zs[:, 0, :],
                      OP.mult, OP.add)
                nb.op("vector", "scalar_tensor_tensor", Zs[:, 1, :],
                      Z[:, cur + 1, :], cst["sel8"][:, r:r + 1], Zs[:, 1, :],
                      OP.mult, OP.add)
            if i_ < 3:
                nxt = 2 - cur
                ag = agT[:, r, :, :]
                nb.op("vector", "tensor_mul", Z[:, nxt, :], ag[:, 0, :], Z[:, cur, :])
                nb.op("vector", "tensor_mul", ctmp, ag[:, 1, :], Z[:, cur + 1, :])
                nb.op("vector", "tensor_add", Z[:, nxt, :], Z[:, nxt, :], ctmp)
                nb.op("vector", "tensor_add", Z[:, nxt, :], Z[:, nxt, :], ag[:, 2, :])
                nb.op("vector", "tensor_mul", Z[:, nxt + 1, :], ag[:, 3, :], Z[:, cur, :])
                nb.op("vector", "tensor_mul", ctmp, ag[:, 4, :], Z[:, cur + 1, :])
                nb.op("vector", "tensor_add", Z[:, nxt + 1, :], Z[:, nxt + 1, :], ctmp)
                nb.op("vector", "tensor_add", Z[:, nxt + 1, :], Z[:, nxt + 1, :], ag[:, 5, :])
                cur = nxt

    GPs = [Pc[i][:, :, 7:127:8] for i in range(6)]
    Tg = P(o_stp, 2, 16, 16)
    ct2 = P(o_stp + 1024, 16, 15)
    Sb0 = Zs[:, 0, :].unsqueeze(2).broadcast_to([128, 16, 15])
    Sb1 = Zs[:, 1, :].unsqueeze(2).broadcast_to([128, 16, 15])
    nb.op("vector", "tensor_copy", Tg[:, 0, :, 0:1], Zs[:, 0, :].unsqueeze(2))
    nb.op("vector", "tensor_copy", Tg[:, 1, :, 0:1], Zs[:, 1, :].unsqueeze(2))
    nb.op("vector", "tensor_mul", Tg[:, 0, :, 1:16], GPs[0], Sb0)
    nb.op("vector", "tensor_mul", ct2, GPs[1], Sb1)
    nb.op("vector", "tensor_add", Tg[:, 0, :, 1:16], Tg[:, 0, :, 1:16], ct2)
    nb.op("vector", "tensor_add", Tg[:, 0, :, 1:16], Tg[:, 0, :, 1:16], GPs[2])
    nb.op("vector", "tensor_mul", Tg[:, 1, :, 1:16], GPs[3], Sb0)
    nb.op("vector", "tensor_mul", ct2, GPs[4], Sb1)
    nb.op("vector", "tensor_add", Tg[:, 1, :, 1:16], Tg[:, 1, :, 1:16], ct2)
    nb.op("vector", "tensor_add", Tg[:, 1, :, 1:16], Tg[:, 1, :, 1:16], GPs[5])
    Sin0 = P(o_stp + 2048, 16, 128)
    Sin1 = P(o_stp + 4096, 16, 128)
    ct3 = P(o_stp + 6144, 16, 16)
    T0 = Tg[:, 0, :, :]
    T1 = Tg[:, 1, :, :]
    nb.op("vector", "tensor_copy", Sin0[:, :, 0::8], T0)
    nb.op("vector", "tensor_copy", Sin1[:, :, 0::8], T1)
    for s_ in range(1, 8):
        WG = [Pc[i][:, :, s_ - 1::8] for i in range(6)]
        nb.op("vector", "tensor_mul", Sin0[:, :, s_::8], WG[0], T0)
        nb.op("vector", "tensor_mul", ct3, WG[1], T1)
        nb.op("vector", "tensor_add", Sin0[:, :, s_::8], Sin0[:, :, s_::8], ct3)
        nb.op("vector", "tensor_add", Sin0[:, :, s_::8], Sin0[:, :, s_::8], WG[2])
        nb.op("vector", "tensor_mul", Sin1[:, :, s_::8], WG[3], T0)
        nb.op("vector", "tensor_mul", ct3, WG[4], T1)
        nb.op("vector", "tensor_add", Sin1[:, :, s_::8], Sin1[:, :, s_::8], ct3)
        nb.op("vector", "tensor_add", Sin1[:, :, s_::8], Sin1[:, :, s_::8], WG[5])
    nb.op("vector", "nop", inc="DV")
    nb.barrier()

    S0s = P(0, 2048)
    S1s = P(2048, 2048)
    for dst, srct in ((S0s, Sin0), (S1s, Sin1)):
        for cg in range(4):
            bi = 4 + cg % 2
            for ci in range(4):
                cc = cg * 4 + ci
                nb.op("tensor", "transpose", pb[bi][:, ci * 128:(ci + 1) * 128],
                      srct[:, cc, :], ident,
                      waits=(bank_free[bi],) if ci == 0 else (),
                      inc="PE" if ci == 3 else None)
            nb.wait("scalar", "PE", nb.now("PE"))
            ac = nb.op("scalar", "activation",
                       dst.rearrange("p (a b) -> p a b", a=16)
                       [:, cg * 4:(cg + 1) * 4, :],
                       pb[bi].rearrange("p (a b) -> p a b", a=4), AF.Copy,
                       inc="AC")
            bank_free[bi] = ("AC", ac)
    nb.barrier()

    o_ysum, o_y2 = 4096, 6144
    o_k0c, o_k1c, o_vc, o_q0c, o_q1c = 8192, 10240, 12288, 14336, 16384
    o_t1, o_t2 = 18432, 20480
    o_stpC = 22528
    o_ybuf = 43008
    o_phpC = 47104
    ysum = P(o_ysum, 2048); y2sum = P(o_y2, 2048)
    k0c = P(o_k0c, 2048); k1c = P(o_k1c, 2048); vc = P(o_vc, 2048)
    q0c = P(o_q0c, 2048); q1c = P(o_q1c, 2048)
    t1 = P(o_t1, 2048); t2 = P(o_t2, 2048)
    stpC = P(o_stpC, 10, 2048)
    ybuf = P(o_ybuf, 2, 2048)
    phpC = P(o_phpC, 2, 256)

    nb.op("vector", "memset", ysum, 0.0)
    nb.op("vector", "memset", y2sum, 0.0, inc="DV")
    dv_stepC = {}
    st_y = {0: 0, 1: 0}
    for j in range(CH):
        par = j % 2
        w_free = (("DV", dv_stepC[j - 2]),) if j >= 2 else ()
        nb.load(stpC[:, 0 + par, :], spj["K0"][:, j, :], waits=w_free)
        nb.load(stpC[:, 2 + par, :], spj["K1"][:, j, :])
        nb.load(stpC[:, 4 + par, :], spj["U"][:, j, :])
        nb.load(stpC[:, 6 + par, :], spj["Q0"][:, j, :])
        nb.load(stpC[:, 8 + par, :], spj["Q1"][:, j, :])
        nb.load(phpC[:, par, :], sphj[:, j, :])
        ldj = nb.now("LD")
        nb.wait("vector", "LD", ldj)
        nb.op("vector", "tensor_mul", vw(k0c), vw(stpC[:, 0 + par, :]), phb(phpC, par, 0))
        nb.op("vector", "tensor_mul", vw(k1c), vw(stpC[:, 2 + par, :]), phb(phpC, par, 1))
        nb.op("vector", "tensor_mul", vw(vc), vw(stpC[:, 4 + par, :]), phb(phpC, par, 7))
        nb.op("vector", "tensor_mul", vw(t1), vw(S1s), phb(phpC, par, 5))
        nb.op("vector", "tensor_mul", vw(t2), vw(S0s), phb(phpC, par, 5))
        nb.op("vector", "tensor_mul", vw(S0s), vw(S0s), phb(phpC, par, 4))
        nb.op("vector", "tensor_add", S0s, S0s, t1)
        nb.op("vector", "tensor_mul", vw(S1s), vw(S1s), phb(phpC, par, 4))
        nb.op("vector", "tensor_sub", S1s, S1s, t2)
        nb.op("vector", "tensor_mul", t1, k0c, S0s)
        nb.op("vector", "tensor_mul", t2, k1c, S1s)
        nb.op("vector", "tensor_add", t1, t1, t2)
        nb.op("vector", "tensor_sub", t1, vc, t1)
        nb.op("vector", "tensor_mul", vw(t1), vw(t1), phb(phpC, par, 6))
        nb.op("vector", "tensor_mul", t2, t1, k0c)
        nb.op("vector", "tensor_add", S0s, S0s, t2)
        nb.op("vector", "tensor_mul", t2, t1, k1c)
        nb.op("vector", "tensor_add", S1s, S1s, t2)
        nb.op("vector", "tensor_mul", vw(q0c), vw(stpC[:, 6 + par, :]), phb(phpC, par, 2))
        nb.op("vector", "tensor_mul", vw(q1c), vw(stpC[:, 8 + par, :]), phb(phpC, par, 3))
        nb.wait("vector", "ST", st_y[par])
        nb.op("vector", "tensor_mul", ybuf[:, par, :], q0c, S0s)
        nb.op("vector", "tensor_mul", t2, q1c, S1s)
        nb.op("vector", "tensor_add", ybuf[:, par, :], ybuf[:, par, :], t2)
        nb.op("vector", "tensor_add", ysum, ysum, ybuf[:, par, :])
        nb.op("vector", "tensor_mul", t2, ybuf[:, par, :], ybuf[:, par, :])
        nb.op("vector", "tensor_add", y2sum, y2sum, t2, inc="DV")
        dv_stepC[j] = nb.now("DV")
        st_y[par] = nb.store(spj["Y"][:, j, :], ybuf[:, par, :],
                             waits=(("DV", dv_stepC[j]),))
    nb.barrier()
    if _STOP_AFTER == "pC":
        _finish(nb)
        return nb, OUT

    for q_ in range(4):
        nb.op("tensor", "matmul", pb[6][0:1, :], ones_col,
              ysum[:, q_ * 512:(q_ + 1) * 512], start=True, stop=True,
              waits=(bank_free[6],), inc="PE")
        nb.wait("scalar", "PE", nb.now("PE"))
        ac = nb.op("scalar", "activation", stat[0:1, q_ * 512:(q_ + 1) * 512],
                   pb[6][0:1, :], AF.Copy, inc="AC")
        bank_free[6] = ("AC", ac)
        nb.op("tensor", "matmul", pb[7][0:1, :], ones_col,
              y2sum[:, q_ * 512:(q_ + 1) * 512], start=True, stop=True,
              waits=(bank_free[7],), inc="PE")
        nb.wait("scalar", "PE", nb.now("PE"))
        ac = nb.op("scalar", "activation",
                   stat[0:1, 2048 + q_ * 512:2048 + (q_ + 1) * 512],
                   pb[7][0:1, :], AF.Copy, inc="AC")
        bank_free[7] = ("AC", ac)
    nb.wait("vector", "AC", nb.now("AC"))
    hsum = P(o_t1, 64)
    nb.op("vector", "tensor_reduce", hsum[0:1, 0:32].unsqueeze(2),
          stat[0:1, 0:2048].rearrange("p (h d) -> p h d", h=32), AX, OP.add)
    nb.op("vector", "tensor_reduce", hsum[0:1, 32:64].unsqueeze(2),
          stat[0:1, 2048:4096].rearrange("p (h d) -> p h d", h=32), AX, OP.add,
          inc="DV")
    nb.store(cc2i, hsum[0:1, :], waits=(("DV", nb.now("DV")),))
    nb.op("gpsimd", "collective_compute", "AllGather", OP.bypass,
          [list(range(NC))], [cc2i], [cc2o],
          waits=(("ST", nb.now("ST")),), inc="CC")
    nb.load(cc2s, cc2o, waits=(("CC", nb.now("CC")),))
    nb.wait("tensor", "LD")
    nb.op("tensor", "matmul", pb[6][0:1, 0:64], cst["bmask"], cc2s,
          start=True, stop=True, waits=(bank_free[6],), inc="PE")
    nb.wait("vector", "PE")
    NTOT = float(HD * S)
    nb.op("vector", "tensor_scalar_mul", stat[0:1, 0:64], pb[6][0:1, 0:64],
          1.0 / NTOT)
    nb.op("vector", "tensor_mul", stat[0:1, 64:96], stat[0:1, 0:32],
          stat[0:1, 0:32])
    nb.op("vector", "tensor_sub", stat[0:1, 32:64], stat[0:1, 32:64],
          stat[0:1, 64:96], inc="DV")
    bank_free[6] = ("DV", nb.now("DV"))
    nb.wait("scalar", "DV", nb.now("DV"))
    nb.op("scalar", "activation", stat[0:1, 32:64], stat[0:1, 32:64],
          AF.Sqrt, bias=1e-5, inc="AC")
    nb.wait("vector", "AC")
    nb.op("vector", "reciprocal", stat[0:1, 32:64], stat[0:1, 32:64], inc="DV")
    for hi_ in range(2):
        for comp_ in range(2):
            dst = stat[0:1, 128 + hi_ * 32 + comp_ * 16:128 + hi_ * 32 + comp_ * 16 + 16]
            src_ = stat[0:1, comp_ * 32 + hi_:comp_ * 32 + hi_ + 31 + 1]\
                .rearrange("p (cc two) -> p cc two", two=2)[:, :, 0:1]\
                .rearrange("p cc two -> p (cc two)")
            nb.op("vector", "tensor_copy", dst, src_)
    nb.op("vector", "nop", inc="DV")
    nb.wait("tensor", "DV", nb.now("DV"))
    nb.op("tensor", "transpose", pb[7][0:64, 0:1], stat[0:1, 128:192],
          ident[0:1, 0:1], waits=(bank_free[7],), inc="PE")
    nb.wait("scalar", "PE", nb.now("PE"))
    ac = nb.op("scalar", "activation", mr64, pb[7][0:64, 0:1], AF.Copy, inc="AC")
    bank_free[7] = ("AC", ac)
    nb.store(sp["MR"], mr64, waits=(("AC", nb.now("AC")),))
    st_mr = nb.now("ST")
    for hi in range(2):
        for comp in range(2):
            base = hi * 32 + comp * 16
            src = sp["MR"][base:base + 16, :].transpose([1, 0])\
                .broadcast_to([64, 16])
            nb.gdma(murc[hi * 64:(hi + 1) * 64, comp, :], src,
                    waits=(("ST", st_mr),))
    nb.barrier()

    o_ysc, o_yg = 0, 8192
    o_gh3, o_x3 = 16384, 18432
    o_x23, o_wo = 22528, 26624
    o_hall = POOL_N - 16384
    assert o_wo + 4096 <= o_hall
    ysc = P(o_ysc, 4, 2048)
    yg = P(o_yg, 16, 512)
    gh3 = P(o_gh3, 4, 512)
    x3 = P(o_x3, 8, 512)
    x23 = P(o_x23, 8, 512)
    h_all = P(o_hall, 8, 2048)
    sq3a = P(o_ysc, 512)

    def wobuf(i):
        return P(o_wo + i * 2048, 16, 128)

    def stats_fm(src_t, sq_ap):
        nb.wait("tensor", "DV", nb.now("DV"))
        for k in range(8):
            nb.op("tensor", "matmul", pb[6][0:1, :], ones_col, src_t[:, k, :],
                  start=(k == 0), stop=(k == 7),
                  waits=(bank_free[6],) if k == 0 else (),
                  inc="PE" if k == 7 else None)
        nb.wait("scalar", "PE", nb.now("PE"))
        ac = nb.op("scalar", "activation", stat[0:1, 0:512], pb[6][0:1, :],
                   AF.Copy, inc="AC")
        bank_free[6] = ("AC", ac)
        pe_prev = 0
        for k in range(8):
            nb.wait("vector", "PE", pe_prev)
            nb.op("vector", "tensor_mul", sq_ap, src_t[:, k, :], src_t[:, k, :],
                  inc="DV")
            nb.wait("tensor", "DV", nb.now("DV"))
            nb.op("tensor", "matmul", pb[7][0:1, :], ones_col, sq_ap,
                  start=(k == 0), stop=(k == 7),
                  waits=(bank_free[7],) if k == 0 else (), inc="PE")
            pe_prev = nb.now("PE")
        nb.wait("scalar", "PE", nb.now("PE"))
        ac = nb.op("scalar", "activation", stat[0:1, 512:1024], pb[7][0:1, :],
                   AF.Copy, inc="AC")
        bank_free[7] = ("AC", ac)
        nb.wait("vector", "AC", nb.now("AC"))
        nb.op("vector", "tensor_scalar_mul", stat[0:1, 1024:1536],
              stat[0:1, 0:512], 1.0 / DM)
        nb.op("vector", "tensor_scalar_mul", stat[0:1, 512:1024],
              stat[0:1, 512:1024], 1.0 / DM)
        nb.op("vector", "tensor_mul", stat[0:1, 1536:2048],
              stat[0:1, 1024:1536], stat[0:1, 1024:1536])
        nb.op("vector", "tensor_sub", stat[0:1, 512:1024],
              stat[0:1, 512:1024], stat[0:1, 1536:2048], inc="DV")
        nb.wait("scalar", "DV", nb.now("DV"))
        nb.op("scalar", "activation", stat[0:1, 512:1024], stat[0:1, 512:1024],
              AF.Sqrt, bias=1e-5, inc="AC")
        nb.wait("vector", "AC")
        nb.op("vector", "reciprocal", stat[0:1, 1536:2048],
              stat[0:1, 512:1024], inc="DV")
        nb.wait("tensor", "DV", nb.now("DV"))
        nb.op("tensor", "matmul", pb[6], ones_row, stat[0:1, 1024:1536],
              start=True, stop=True, waits=(bank_free[6],))
        nb.op("tensor", "matmul", pb[7], ones_row, stat[0:1, 1536:2048],
              start=True, stop=True, waits=(bank_free[7],), inc="PE")
        nb.wait("vector", "PE", nb.now("PE"))

    for tau in range(NT):
        dv_pre = nb.now("DV")
        for blk in range(4):
            nb.load(ysc[:, blk, :],
                    sp["Y"][tau * 512 + blk * 128:tau * 512 + (blk + 1) * 128, :],
                    waits=(("DV", dv_pre), ("PE", nb.now("PE"))) if blk == 0 else ())
        nb.load(x3, xv[:, :, 3 + tau * TT:3 + (tau + 1) * TT])
        ld3 = nb.now("LD")
        for blk in range(4):
            for cg in range(4):
                bi = 4 + cg % 2
                for ci in range(4):
                    cc = cg * 4 + ci
                    nb.op("tensor", "transpose",
                          pb[bi][:, ci * 128:(ci + 1) * 128],
                          ysc[:, blk, cc * 128:(cc + 1) * 128], ident,
                          waits=(bank_free[bi], ("LD", ld3)) if ci == 0 else (),
                          inc="PE" if ci == 3 else None)
                nb.wait("scalar", "PE", nb.now("PE"))
                ac = nb.op("scalar", "activation",
                           yg[:, cg * 4:(cg + 1) * 4, blk * 128:(blk + 1) * 128],
                           pb[bi].rearrange("p (a b) -> p a b", a=4), AF.Copy,
                           inc="AC")
                bank_free[bi] = ("AC", ac)
        ac_yg = nb.now("AC")
        gst = {0: 0, 1: 0}
        for cc in range(16):
            par = cc % 2
            nb.load(gh3[:, par, :],
                    sp["G"][cc * 128:(cc + 1) * 128, tau * TT:(tau + 1) * TT],
                    waits=(("DV", gst[par]),))
            nb.load(gh3[:, 2 + par, :],
                    sp["Hs"][cc * 128:(cc + 1) * 128, tau * TT:(tau + 1) * TT])
            ldgh = nb.now("LD")
            nb.wait("vector", "AC", ac_yg)
            nb.wait("vector", "LD", ldgh)
            nb.op("vector", "tensor_scalar", yg[:, cc, :], yg[:, cc, :],
                  murc[:, 0, cc:cc + 1], murc[:, 1, cc:cc + 1], OP.subtract, OP.mult)
            nb.op("vector", "tensor_mul", yg[:, cc, :], yg[:, cc, :],
                  gh3[:, par, :])
            gst[par] = nb.op("vector", "tensor_add", yg[:, cc, :],
                             yg[:, cc, :], gh3[:, 2 + par, :], inc="DV")
        dv_yg = nb.now("DV")
        for m in range(8):
            wo = wobuf(m % 2)
            nb.load(wo, woutv[:, :, m * 128:(m + 1) * 128],
                    waits=(("PE", wb_free[m % 2]),))
            nb.wait("tensor", "LD")
            nb.wait("tensor", "DV", dv_yg)
            fw = (bank_free[m % 2],)
            for k in range(16):
                nb.op("tensor", "matmul", pb[m % 2], wo[:, k, :], yg[:, k, :],
                      start=(k == 0), stop=(k == 15),
                      waits=fw if k == 0 else (),
                      inc="PE" if k == 15 else None)
            wb_free[m % 2] = nb.now("PE")
            nb.wait("vector", "PE", nb.now("PE"))
            nb.wait("vector", "LD", ld3)
            dv = nb.op("vector", "scalar_tensor_tensor", x3[:, m, :],
                       x3[:, m, :], 2.0, pb[m % 2], OP.mult, OP.add, inc="DV")
            bank_free[m % 2] = ("DV", dv)
        stats_fm(x3, sq3a)
        for k in range(8):
            nb.op("vector", "tensor_sub", x23[:, k, :], x3[:, k, :], pb[6])
            nb.op("vector", "tensor_mul", x23[:, k, :], x23[:, k, :], pb[7],
                  inc="DV" if k == 7 else None)
        dv_x2 = nb.now("DV")
        bank_free[6] = ("DV", dv_x2)
        bank_free[7] = ("DV", dv_x2)
        for k in range(8):
            nb.store(x2v[:, k, tau * TT:(tau + 1) * TT], x23[:, k, :],
                     waits=(("DV", dv_x2),) if k == 0 else ())
        stats_fm(x23, sq3a)
        for k in range(8):
            nb.op("vector", "tensor_sub", sq3a, x23[:, k, :], pb[6])
            nb.op("vector", "tensor_mul", sq3a, sq3a, pb[7])
            nb.op("vector", "tensor_scalar",
                  h_all[:, k, tau * TT:(tau + 1) * TT], sq3a,
                  cst["lng"][:, k:k + 1], cst["lnb"][:, k:k + 1],
                  OP.mult, OP.add, inc="DV" if k == 7 else None)
        bank_free[6] = ("DV", nb.now("DV"))
        bank_free[7] = ("DV", nb.now("DV"))
    nb.barrier()
    if _DEBUG_SPILL and globals().get("_DUMP_H", False):
        for k in range(8):
            nb.store(sp["Y"][k * 128:(k + 1) * 128, :], h_all[:, k, :])
        nb.barrier()
    if _STOP_AFTER == "p3a":
        _finish(nb)
        return nb, OUT

    o_fq = 0
    o_w1d = 2048
    o_sc1 = 3072
    o_s1o = 5120
    fqb = [P(o_fq + i * 1024, 8, 128, i32=True) for i in range(2)]
    w1d = P(o_w1d, 8, 128)
    sc1b = [P(o_sc1 + i * 1024, 8, 128) for i in range(2)]
    s1o = [P(o_s1o + i * 512, 512) for i in range(2)]
    fq_free = {0: 0, 1: 0}
    s1st = {0: 0, 1: 0}
    for m in range(32):
        par = m % 2
        nb.load(fqb[par], fc1qv[:, :, m * 128:(m + 1) * 128],
                waits=(("DV", fq_free[par]),))
        nb.gdma(sc1b[par],
                ins["fc1s"][:, m * 128:(m + 1) * 128].unsqueeze(0)
                .broadcast_to([128, 8, 128]),
                waits=(("DV", fq_free[par]),))
        ldm = nb.now("LD")
        gdm = nb.now("GD")
        nb.wait("vector", "LD", ldm)
        nb.wait("vector", "GD", gdm)
        nb.wait("vector", "PE", nb.now("PE"))
        for g in range(8):
            nb.op("vector", "scalar_tensor_tensor", w1d[:, g, :],
                  fqb[par][:, g, :], -128.0, sc1b[par][:, g, :],
                  OP.add, OP.mult, inc="DV" if g == 7 else None)
        fq_free[par] = nb.now("DV")
        dv_w1d = nb.now("DV")
        if _DEBUG_SPILL and m == 0 and globals().get("_DUMP_H", False):
            for g in range(8):
                nb.store(sp["Y"][1024 + g * 128:1024 + (g + 1) * 128, 0:128],
                         w1d[:, g, :], waits=(("DV", dv_w1d),))
        for tau in range(NT):
            bi = tau % 2
            fw = (bank_free[bi], ("DV", dv_w1d))
            for g in range(8):
                nb.op("tensor", "matmul", pb[bi], w1d[:, g, :],
                      h_all[:, g, tau * TT:(tau + 1) * TT],
                      start=(g == 0), stop=(g == 7),
                      waits=fw if g == 0 else (),
                      inc="PE" if g == 7 else None)
            nb.wait("scalar", "PE", nb.now("PE"))
            nb.wait("scalar", "ST", s1st[bi])
            ac = nb.op("scalar", "activation", s1o[bi], pb[bi], AF.Silu,
                       bias=cst["fc1b"][:, m:m + 1], inc="AC")
            bank_free[bi] = ("AC", ac)
            s1st[bi] = nb.store(
                sp["Ssp"][m * 128:(m + 1) * 128, tau * TT:(tau + 1) * TT],
                s1o[bi], waits=(("AC", ac),))
        if _STOP_AFTER == f"fc1m{m}":
            break
    nb.barrier()
    if (_STOP_AFTER or "").startswith("fc1m"):
        _finish(nb)
        return nb, OUT

    o_f2d = 0
    o_fq2 = 16384
    o_sc2 = 17408
    o_sst = 18432
    o_oall = POOL_N - 16384
    f2d = P(o_f2d, 32, 512)
    fq2b = [P(o_fq2 + i * 512, 512, i32=True) for i in range(2)]
    sc2b = [P(o_sc2 + i * 512, 512) for i in range(2)]
    sstb = [P(o_sst + i * 512, 512) for i in range(2)]
    o_all = P(o_oall, 8, 2048)
    fq2_free = {0: 0, 1: 0}
    sst_free = {0: 0, 1: 0}
    for half in range(2):
        for m in range(32):
            par = m % 2
            nb.load(fq2b[par], fc2qv[:, m, half * 512:(half + 1) * 512],
                    waits=(("DV", fq2_free[par]),))
            nb.gdma(sc2b[par],
                    ins["fc2s"][m:m + 1, half * 512:(half + 1) * 512]
                    .broadcast_to([128, 512]),
                    waits=(("DV", fq2_free[par]),))
            nb.wait("vector", "LD", nb.now("LD"))
            nb.wait("vector", "GD", nb.now("GD"))
            fq2_free[par] = nb.op("vector", "scalar_tensor_tensor",
                                  f2d[:, m, :], fq2b[par], -128.0, sc2b[par],
                                  OP.add, OP.mult, inc="DV")
        dv_f2d = nb.now("DV")
        for tau in range(NT):
            for m in range(32):
                par = m % 2
                nb.load(sstb[par],
                        sp["Ssp"][m * 128:(m + 1) * 128, tau * TT:(tau + 1) * TT],
                        waits=(("PE", sst_free[par]),))
                nb.wait("tensor", "LD")
                if m == 0:
                    nb.wait("tensor", "DV", dv_f2d)
                    for n1 in range(4):
                        nb.wait("tensor", *bank_free[n1])
                for n1 in range(4):
                    nb.op("tensor", "matmul", pb[n1],
                          f2d[:, m, n1 * 128:(n1 + 1) * 128], sstb[par],
                          start=(m == 0), stop=(m == 31),
                          inc="PE" if n1 == 3 else None)
                sst_free[par] = nb.now("PE")
            nb.wait("scalar", "PE", nb.now("PE"))
            for n1 in range(4):
                nch = half * 4 + n1
                ac = nb.op("scalar", "activation",
                           o_all[:, nch, tau * TT:(tau + 1) * TT], pb[n1],
                           AF.Identity, scale=0.5, bias=fc2b05[:, nch:nch + 1],
                           inc="AC")
                bank_free[n1] = ("AC", ac)
    nb.barrier()
    if _DEBUG_SPILL:
        for k in range(8):
            nb.store(sp["Y"][k * 128:(k + 1) * 128, :], o_all[:, k, :])
        nb.barrier()

    x2l = P(0, 8, 512)
    sq4 = P(4096, 512)
    for tau in range(NT):
        nb.load(x2l, x2v[:, :, tau * TT:(tau + 1) * TT],
                waits=(("DV", nb.now("DV")), ("PE", nb.now("PE")),
                       ("AC", nb.now("AC"))))
        ld4 = nb.now("LD")
        nb.wait("vector", "LD", ld4)
        for k in range(8):
            nb.op("vector", "scalar_tensor_tensor", x2l[:, k, :], x2l[:, k, :],
                  2.0, o_all[:, k, tau * TT:(tau + 1) * TT], OP.mult, OP.add,
                  inc="DV" if k == 7 else None)
        stats_fm(x2l, sq4)
        for k in range(8):
            nb.op("vector", "tensor_sub", sq4, x2l[:, k, :], pb[6])
            nb.op("vector", "tensor_mul",
                  o_all[:, k, tau * TT:(tau + 1) * TT], sq4, pb[7], inc="DV")
            nb.store(outv[:, k, tau * TT:(tau + 1) * TT],
                     o_all[:, k, tau * TT:(tau + 1) * TT],
                     waits=(("DV", nb.now("DV")),))
        bank_free[6] = ("DV", nb.now("DV"))
        bank_free[7] = ("DV", nb.now("DV"))
    nb.barrier()
    _finish(nb)
    return nb, OUT


def _finish(nb):
    for s_ in ("LD", "ST", "GD", "CC"):
        if nb.now(s_):
            nb.wait("sync", s_, nb.now(s_))
    nb.check()


def emit_all(nb):
    nc = nb.nc
    import contextlib
    with contextlib.ExitStack() as ctx:
        for s_ in ("LD", "ST", "GD", "PE", "DV", "AC", "CC", "BAR"):
            nb.sems[s_] = ctx.enter_context(nc.semaphore(s_))
        block = ctx.enter_context(nc.Block())
        for eng in nb.ENGS:
            thunks = nb.prog[eng]
            getattr(block, eng)(lambda e, th=thunks: [t(e) for t in th])
    return nc


def kernel(**inputs):
    cores = _host_stage(inputs)
    nb, OUT = build_nc()
    nc = emit_all(nb)
    names = [n for n, _, _ in _IN_SPECS]
    in_maps = [{n: cores[c][n] for n in names} for c in range(NC)]
    res = run_bass_kernel_spmd(nc, in_maps, core_ids=list(range(NC)))
    out = np.zeros((B, S, DM), np.float32)
    for c in range(NC):
        b, q = c // 4, c % 4
        out[b, q * SC:(q + 1) * SC, :] = np.asarray(res.results[c]["OUT"]).T
    return out


# version 02
# speedup vs baseline: 1.0037x; 1.0037x over previous
import numpy as np
import concourse.bass as bass
import concourse.mybir as mybir
from concourse.bass_utils import run_bass_kernel_spmd

F32 = mybir.dt.float32
I32 = mybir.dt.int32
AF = mybir.ActivationFunctionType
OP = mybir.AluOpType
AX = mybir.AxisListType.X

B, S, DM, DI, H = 2, 8192, 1024, 2048, 32
HD = DI // H
NC = 8
SC = S // 4
TT = 512
NT = SC // TT
CH = 16
NSUB = SC // CH

_DEBUG_SPILL = False
_STOP_AFTER = None
POOL_N = 48128


def _rope_fp32():
    r = (np.float32(1.0) / np.power(np.float32(10000.0),
         np.arange(H, dtype=np.float32) / np.float32(H))).astype(np.float32)
    u = r.view(np.uint32).copy()
    u[5] = 0x3e72d423
    u[27] = 0x39dd1725
    return u.view(np.float32)


def _to_chunkcol(v, nch):
    return np.ascontiguousarray(v.reshape(nch, 128).T.astype(np.float32))


def _host_stage(inp):
    f = np.float32
    st = {}
    Wp = np.asarray(inp["L0_attn_in_proj_W"], f)
    perm = np.concatenate([
        3 * DI + np.arange(DI),
        np.arange(DI),
        DI + 2 * np.arange(DI),
        DI + 1 + 2 * np.arange(DI),
    ])
    st["Win"] = np.ascontiguousarray(Wp[perm, :].T)
    st["binL"] = _to_chunkcol(np.asarray(inp["L0_attn_in_proj_b"], f)[perm], 64)
    dyn_W = np.asarray(inp["L0_attn_dyn_W"], f)
    selB_W = np.asarray(inp["L0_attn_selB_W"], f)
    selC_W = np.asarray(inp["L0_attn_selC_W"], f)
    rows = [selB_W[0::2], selB_W[1::2], selC_W[0::2], selC_W[1::2],
            dyn_W[0:32], dyn_W[32:64], dyn_W[64:96],
            np.asarray(inp["L0_attn_seldt_W"], f),
            np.asarray(inp["L0_attn_beta_W"], f),
            np.asarray(inp["L0_attn_rg_W"], f),
            np.asarray(inp["L0_attn_ug_W"], f),
            np.zeros((32, DI), f)]
    st["Wsm"] = np.ascontiguousarray(np.concatenate(rows, 0).T)
    dyn_b = np.asarray(inp["L0_attn_dyn_b"], f)
    bs = [np.zeros(32, f)] * 4 + [dyn_b[0:32], dyn_b[32:64], dyn_b[64:96],
          np.zeros(32, f), np.asarray(inp["L0_attn_beta_b"], f),
          np.asarray(inp["L0_attn_rg_b"], f),
          np.asarray(inp["L0_attn_ug_b"], f), np.zeros(32, f)]
    st["bsmL"] = _to_chunkcol(np.concatenate(bs, 0), 3)
    QW = np.asarray(inp["L0_attn_Q_W"], f)
    permq = np.concatenate([2 * np.arange(DI), 1 + 2 * np.arange(DI)])
    st["WQ"] = np.ascontiguousarray(QW[permq, :].T)
    st["Wout"] = np.ascontiguousarray(np.asarray(inp["L0_attn_out_W"], f).T)
    st["convw"] = np.ascontiguousarray(
        np.asarray(inp["L0_attn_conv_w"], f).reshape(16, 128, 4).transpose(1, 0, 2))
    st["convb"] = _to_chunkcol(np.asarray(inp["L0_attn_conv_b"], f), 16)
    st["rmsw"] = _to_chunkcol(np.asarray(inp["L0_attn_rms_w"], f), 8)
    st["gnw"] = _to_chunkcol(np.asarray(inp["L0_attn_gn_w"], f), 16)
    st["gnb"] = _to_chunkcol(np.asarray(inp["L0_attn_gn_b"], f), 16)
    st["Dv"] = _to_chunkcol(np.asarray(inp["L0_attn_D"], f), 16)
    st["dtc"] = np.ascontiguousarray(np.asarray(inp["L0_attn_dt_c"], f).reshape(32, 1))
    st["lng"] = _to_chunkcol(np.asarray(inp["L0_ffn_ln_gamma"], f), 8)
    st["lnb"] = _to_chunkcol(np.asarray(inp["L0_ffn_ln_beta"], f), 8)
    st["fc1q"] = np.ascontiguousarray(np.asarray(inp["L0_ffn_fc1_q"], np.int32))
    st["fc2q"] = np.ascontiguousarray(np.asarray(inp["L0_ffn_fc2_q"], np.int32))
    st["fc1s"] = np.ascontiguousarray(np.asarray(inp["L0_ffn_fc1_scales"], f))
    st["fc2s"] = np.ascontiguousarray(np.asarray(inp["L0_ffn_fc2_scales"], f))
    st["fc1b"] = _to_chunkcol(np.asarray(inp["L0_ffn_fc1_bias"], f), 32)
    st["fc2b"] = _to_chunkcol(np.asarray(inp["L0_ffn_fc2_bias"], f), 8)
    st["ident"] = np.eye(128, dtype=f)
    sel = np.zeros((128, 160), f)
    for m in range(32):
        sel[m, m] = 1.0
        sel[32 + m, 32 + m] = 1.0
        sel[64 + m, 32 + m] = 1.0
        sel[64 + m, 64 + m] = 1.0
        sel[32 + m, 96 + m] = 1.0
        sel[96 + m, 128 + m] = 1.0
    st["SELS"] = sel

    rope = _rope_fp32()
    x = np.asarray(inp["L0_attn_x"], f)
    cores = []
    for c in range(NC):
        b, q = c // 4, c % 4
        t0 = q * SC
        xf = np.zeros((DM, 3 + SC), f)
        xf[:, 3:] = x[b, t0:t0 + SC, :].T
        if q > 0:
            xf[:, :3] = x[b, t0 - 3:t0, :].T
        d = dict(st)
        d["x_fm"] = np.ascontiguousarray(xf)
        d["halom"] = np.full((128, 1), 0.0 if q == 0 else 1.0, f)
        t = np.arange(t0, t0 + SC, dtype=f)
        pr = (t[None, :] * rope[:, None]).astype(f)
        d["posrope"] = np.ascontiguousarray(
            pr.reshape(32, 4, 512).transpose(1, 0, 2).reshape(128, 512))
        selc = np.zeros((128, 8), f); selc[:, c] = 1.0
        d["sel8"] = selc
        bm = np.zeros((8, 1), f); bm[4 * b:4 * b + 4, 0] = 1.0
        d["bmask"] = bm
        cores.append(d)
    return cores


_IN_SPECS = [
    ("x_fm", [DM, 3 + SC], F32), ("Win", [DM, 4 * DI], F32),
    ("binL", [128, 64], F32), ("Wsm", [DI, 384], F32), ("bsmL", [128, 3], F32),
    ("WQ", [DI, 2 * DI], F32), ("Wout", [DI, DM], F32),
    ("convw", [128, 16, 4], F32), ("convb", [128, 16], F32),
    ("rmsw", [128, 8], F32), ("gnw", [128, 16], F32), ("gnb", [128, 16], F32),
    ("Dv", [128, 16], F32), ("dtc", [32, 1], F32),
    ("lng", [128, 8], F32), ("lnb", [128, 8], F32),
    ("fc1q", [DM, 4096], I32), ("fc2q", [4096, DM], I32),
    ("fc1s", [8, 4096], F32), ("fc2s", [32, DM], F32),
    ("fc1b", [128, 32], F32), ("fc2b", [128, 8], F32),
    ("ident", [128, 128], F32), ("SELS", [128, 160], F32),
    ("halom", [128, 1], F32), ("posrope", [128, 512], F32),
    ("sel8", [128, 8], F32), ("bmask", [8, 1], F32),
]
_CONST_SKIP = {"x_fm", "Win", "Wsm", "WQ", "Wout", "fc1q", "fc2q", "fc1s",
               "fc2s", "posrope"}


class KB:
    ENGS = ("sync", "tensor", "vector", "scalar", "gpsimd")

    def __init__(self):
        self.nc = bass.Bass("TRN2", target_bir_lowering=False, debug=False,
                            detect_race_conditions=False)
        self.prog = {e: [] for e in self.ENGS}
        self.cnt = {}
        self.sems = {}
        self.waits_emitted = []

    def now(self, sem):
        return self.cnt.get(sem, 0)

    _DRAIN_ENGS = ("vector", "scalar")
    _NODRAIN_METHODS = ("dma_start", "nop", "drain", "wait_ge")

    def op(self, eng, method, *args, waits=(), inc=None, incn=1, drain=None,
           **kw):
        sems = self.sems
        wl = tuple(w for w in waits if w[1] > 0)
        self.waits_emitted.extend(wl)
        if inc is not None:
            self.cnt[inc] = self.cnt.get(inc, 0) + incn
        if drain is None:
            drain = eng in self._DRAIN_ENGS and method not in self._NODRAIN_METHODS

        def thunk(e, method=method, args=args, kw=kw, wl=wl, inc=inc,
                  incn=incn, drain=drain):
            if drain:
                e.drain()
            for s_, v_ in wl:
                e.wait_ge(sems[s_], v_)
            ret = getattr(e, method)(*args, **kw)
            if inc is not None:
                ret.then_inc(sems[inc], incn)
        self.prog[eng].append(thunk)
        return self.cnt.get(inc, 0) if inc is not None else None

    def wait(self, eng, sem, val=None):
        v = self.now(sem) if val is None else val
        if v <= 0:
            return
        self.waits_emitted.append((sem, v))
        sems = self.sems

        def thunk(e, sem=sem, v=v):
            e.wait_ge(sems[sem], v)
        self.prog[eng].append(thunk)

    def barrier(self):
        for e in self.ENGS:
            self.op(e, "nop", inc="BAR")
        tgt = self.now("BAR")
        for e in self.ENGS:
            self.wait(e, "BAR", tgt)

    def load(self, out_ap, in_ap, waits=()):
        return self.op("sync", "dma_start", waits=waits, inc="LD", incn=16,
                       out=out_ap, in_=in_ap)

    def store(self, out_ap, in_ap, waits=()):
        return self.op("scalar", "dma_start", waits=waits, inc="ST", incn=16,
                       out=out_ap, in_=in_ap)

    def gdma(self, out_ap, in_ap, waits=()):
        return self.op("gpsimd", "dma_start", waits=waits, inc="GD", incn=16,
                       out=out_ap, in_=in_ap)

    def check(self):
        for s_, v_ in self.waits_emitted:
            assert v_ <= self.cnt.get(s_, 0), \
                f"wait {s_}>={v_} never reached (final {self.cnt.get(s_, 0)})"


def build_nc():
    nb = KB()
    nc = nb.nc
    ins = {}
    for name, shape, dt in _IN_SPECS:
        ins[name] = nc.dram_tensor(name, shape, dt, kind="ExternalInput").ap()
    OUT = nc.dram_tensor("OUT", [DM, SC], F32, kind="ExternalOutput").ap()
    spkind = "ExternalOutput" if _DEBUG_SPILL else "Internal"
    sp = {}
    for name, shape in [("K0", [SC, DI]), ("K1", [SC, DI]), ("U", [SC, DI]),
                        ("Q0", [SC, DI]), ("Q1", [SC, DI]), ("PH", [SC, 256]),
                        ("Y", [SC, DI]), ("G", [DI, SC]), ("Hs", [DI, SC]),
                        ("X2", [DM, SC]), ("Ssp", [4096, SC]), ("MR", [64, 1])]:
        sp[name] = nc.dram_tensor("SP" + name, shape, F32, kind=spkind).ap()
    cc1i = nc.dram_tensor("cc1i", [128, 96], F32, kind="Internal").ap()
    cc1o = nc.dram_tensor("cc1o", [128 * NC, 96], F32, kind="Internal").ap()
    cc2i = nc.dram_tensor("cc2i", [1, 64], F32, kind="Internal").ap()
    cc2o = nc.dram_tensor("cc2o", [NC, 64], F32, kind="Internal").ap()

    cst = {}
    for name, shape, dt in _IN_SPECS:
        if name in _CONST_SKIP:
            continue
        cst[name] = nc.alloc_sbuf_tensor("c_" + name, list(shape), dt).ap()
    ones_col = nc.alloc_sbuf_tensor("ones_col", [128, 1], F32).ap()
    ones_row = nc.alloc_sbuf_tensor("ones_row", [1, 128], F32).ap()
    spdtc = nc.alloc_sbuf_tensor("spdtc", [32, 1], F32).ap()
    dwork = nc.alloc_sbuf_tensor("dwork", [32, 2], F32).ap()
    fc2b05 = nc.alloc_sbuf_tensor("fc2b05", [128, 8], F32).ap()
    stat = nc.alloc_sbuf_tensor("statrow", [1, 4096], F32).ap()
    murc = nc.alloc_sbuf_tensor("murc", [128, 2, 16], F32).ap()
    cc2s = nc.alloc_sbuf_tensor("cc2s", [8, 64], F32).ap()
    mr64 = nc.alloc_sbuf_tensor("mr64", [64, 1], F32).ap()
    constv = nc.alloc_sbuf_tensor("constv", [128, 4], F32).ap()
    pool = nc.alloc_sbuf_tensor("pool", [128, POOL_N], F32).ap()
    pool_i = pool.bitcast(I32)

    def P(off, *shape, i32=False):
        n = int(np.prod(shape))
        base = pool_i if i32 else pool
        v = base[:, off:off + n]
        if len(shape) > 1:
            letters = "abcdefg"[:len(shape)]
            v = v.rearrange(f"p ({' '.join(letters)}) -> p {' '.join(letters)}",
                            **{k: int(s_) for k, s_ in zip(letters, shape)})
        return v

    pb = [nc.alloc_psum_tensor(f"pb{i}", [128, 512], F32).ap() for i in range(8)]

    xv = ins["x_fm"].rearrange("(c p) t -> p c t", p=128)
    winv = ins["Win"].rearrange("(c p) m -> p c m", p=128)
    wsmv = ins["Wsm"].rearrange("(c p) m -> p c m", p=128)
    wqv = ins["WQ"].rearrange("(c p) m -> p c m", p=128)
    woutv = ins["Wout"].rearrange("(c p) m -> p c m", p=128)
    fc1qv = ins["fc1q"].rearrange("(c p) m -> p c m", p=128)
    fc2qv = ins["fc2q"].rearrange("(c p) m -> p c m", p=128)
    outv = OUT.rearrange("(c p) t -> p c t", p=128)
    x2v = sp["X2"].rearrange("(c p) t -> p c t", p=128)
    ident = cst["ident"]

    spj = {k: sp[k].rearrange("(p j) c -> p j c", j=CH) for k in
           ("K0", "K1", "U", "Q0", "Q1", "Y")}
    sphj = sp["PH"].rearrange("(p j) c -> p j c", j=CH)
    spblk = {k: sp[k].rearrange("(T r i) c -> T r i c", r=4, i=128) for k in
             ("K0", "K1", "U", "Q0", "Q1")}
    sphblk = sp["PH"].rearrange("(T r i) c -> T i r c", r=4, i=128)

    bank_free = {i: ("AC", 0) for i in range(8)}
    wb_free = {0: 0, 1: 0}
    spst_free = [0, 0]

    for name in cst:
        nb.load(cst[name], ins[name])
    nb.op("vector", "memset", ones_col, 1.0)
    nb.op("vector", "memset", ones_row, 1.0)
    for ci_, cval_ in enumerate((1e-6, 1e-5, 0.5, -1.0)):
        nb.op("vector", "memset", constv[:, ci_:ci_ + 1], cval_)
        nc.const_aps.register(F32, cval_, constv[:, ci_:ci_ + 1])
    nb.wait("scalar", "LD")
    nb.op("scalar", "activation", dwork[:, 0:1], cst["dtc"], AF.Abs)
    nb.op("scalar", "activation", dwork[:, 0:1], dwork[:, 0:1], AF.Exp, scale=-1.0)
    nb.op("scalar", "activation", dwork[:, 0:1], dwork[:, 0:1], AF.Ln, bias=1.0)
    nb.op("scalar", "activation", dwork[:, 1:2], cst["dtc"], AF.Relu, inc="AC")
    nb.wait("vector", "AC")
    nb.op("vector", "tensor_add", spdtc, dwork[:, 0:1], dwork[:, 1:2])
    nb.op("vector", "tensor_scalar_mul", fc2b05, cst["fc2b"], 0.5, inc="DV")
    nb.barrier()
    if _STOP_AFTER == "consts":
        _finish(nb)
        return nb, OUT

    off = [0]

    def take(n):
        o = off[0]
        off[0] += n
        assert off[0] <= POOL_N, f"pool overflow {off[0]}"
        return o

    o_x = take(8 * TT)
    o_V = take(16 * 515); o_u = take(16 * TT)
    o_g = take(2 * TT); o_gh = take(2 * TT)
    o_sm = take(3 * TT); o_kst = take(4 * TT)
    o_spst = take(2 * 2048); o_wb = take(2 * 4096)
    o_ph = take(4 * 256)
    o_hd = take(12 * 512)
    o_vh = take(16 * 3); o_hx = take(8 * 3); o_hx2 = take(8 * 3)
    o_pr = take(512)

    x_t = P(o_x, 8, TT)
    xn_t = x_t
    V_t = P(o_V, 16, 515); u_t = P(o_u, 16, TT)
    g_t = P(o_g, 2, TT); gh_t = P(o_gh, 2, TT)
    sm_t = P(o_sm, 3, TT); kst = P(o_kst, 4, TT)
    spst = [P(o_spst + i * 512, 512) for i in range(2)]
    ph_t = P(o_ph, 4, 256)
    hdw = P(o_hd, 12, 512)
    vh_t = P(o_vh, 16, 3); xh_t = P(o_hx, 8, 3); xnh_t = P(o_hx2, 8, 3)
    pr_t = P(o_pr, 512)

    def wbuf(i, kch, m):
        return P(o_wb + i * 4096, kch, m)

    def hd(i, rows=128):
        return hdw[0:rows, i, :]

    def spill_group(tau, blkview, cgroup, src4, src_waits):
        for blk in range(4):
            par = (cgroup * 4 + blk) % 2
            bi = 4 + blk % 2
            w0 = tuple(src_waits) + (bank_free[bi],)
            for ci in range(4):
                nb.op("tensor", "transpose", pb[bi][:, ci * 128:(ci + 1) * 128],
                      src4[ci][:, blk * 128:(blk + 1) * 128], ident,
                      waits=w0 if ci == 0 else (),
                      inc="PE" if ci == 3 else None)
            pe_t = nb.now("PE")
            nb.wait("scalar", "PE", pe_t)
            nb.wait("scalar", "ST", spst_free[par])
            ac = nb.op("scalar", "activation", spst[par], pb[bi], AF.Copy,
                       inc="AC")
            bank_free[bi] = ("AC", ac)
            spst_free[par] = nb.store(
                blkview[tau, blk, :, cgroup * 512:(cgroup + 1) * 512], spst[par],
                waits=(("AC", ac),))
        return nb.now("PE")

    nb.load(xh_t, xv[:, :, 0:3])
    nb.wait("vector", "LD")
    nb.op("vector", "tensor_mul", xnh_t, xh_t, xh_t, inc="DV")
    nb.wait("tensor", "DV")
    for k in range(8):
        nb.op("tensor", "matmul", pb[6][0:1, 0:3], ones_col, xnh_t[:, k, :],
              start=(k == 0), stop=(k == 7), inc="PE" if k == 7 else None)
    nb.wait("scalar", "PE")
    ac = nb.op("scalar", "activation", stat[0:1, 0:3], pb[6][0:1, 0:3],
               AF.Sqrt, bias=1e-6, scale=1.0 / DM, inc="AC")
    bank_free[6] = ("AC", ac)
    nb.wait("vector", "AC")
    nb.op("vector", "reciprocal", stat[0:1, 4:7], stat[0:1, 0:3], inc="DV")
    nb.wait("tensor", "DV")
    nb.op("tensor", "matmul", pb[7][:, 0:3], ones_row, stat[0:1, 4:7],
          start=True, stop=True, inc="PE")
    nb.wait("vector", "PE")
    for k in range(8):
        nb.op("vector", "scalar_tensor_tensor", xnh_t[:, k, :], xh_t[:, k, :],
              cst["rmsw"][:, k:k + 1], pb[7][:, 0:3], OP.mult, OP.mult,
              inc="DV" if k == 7 else None)
    bank_free[7] = ("DV", nb.now("DV"))
    nb.wait("tensor", "DV")
    for pn in range(8):
        wpan = wbuf(pn % 2, 8, 256)
        nb.load(wpan, winv[:, :, pn * 256:(pn + 1) * 256],
                waits=(("PE", wb_free[pn % 2]),))
        nb.wait("tensor", "LD")
        for m2 in range(2):
            m = pn * 2 + m2
            bi = m % 2
            for k in range(8):
                nb.op("tensor", "matmul", pb[bi][:, 0:3],
                      wpan[:, k, m2 * 128:(m2 + 1) * 128], xnh_t[:, k, :],
                      start=(k == 0), stop=(k == 7),
                      waits=(bank_free[bi],) if k == 0 else (),
                      inc="PE" if k == 7 else None)
            nb.wait("vector", "PE")
            dv = nb.op("vector", "tensor_scalar", vh_t[:, m, :],
                       pb[bi][:, 0:3], cst["binL"][:, m:m + 1],
                       cst["halom"], OP.add, OP.mult, inc="DV")
            bank_free[bi] = ("DV", dv)
        wb_free[pn % 2] = nb.now("PE")
    halo_dv = nb.now("DV")
    if _STOP_AFTER == "halo":
        nb.barrier()
        _finish(nb)
        return nb, OUT

    g_store = {0: 0, 1: 0}
    for tau in range(NT):
        c0 = 3 + tau * TT
        nb.load(x_t, xv[:, :, c0:c0 + TT],
                waits=(("DV", nb.now("DV")), ("PE", nb.now("PE")),
                       ("AC", nb.now("AC"))))
        nb.load(pr_t[0:32, :],
                ins["posrope"].rearrange("(T h) t -> T h t", h=32)[tau])
        ld_x = nb.now("LD")
        sq = [hd(0), hd(1)]
        pe_sq = {}
        nb.wait("vector", "LD", ld_x)
        for k in range(8):
            if k >= 2:
                nb.wait("vector", "PE", pe_sq[k - 2])
            nb.op("vector", "tensor_mul", sq[k % 2], x_t[:, k, :], x_t[:, k, :],
                  inc="DV")
            nb.wait("tensor", "DV")
            nb.op("tensor", "matmul", pb[6][0:1, :], ones_col, sq[k % 2],
                  start=(k == 0), stop=(k == 7),
                  waits=(bank_free[6],) if k == 0 else (), inc="PE")
            pe_sq[k] = nb.now("PE")
        nb.wait("scalar", "PE")
        ac = nb.op("scalar", "activation", stat[0:1, 0:TT], pb[6][0:1, :],
                   AF.Sqrt, bias=1e-6, scale=1.0 / DM, inc="AC")
        bank_free[6] = ("AC", ac)
        nb.wait("vector", "AC")
        nb.op("vector", "reciprocal", stat[0:1, 512:512 + TT],
              stat[0:1, 0:TT], inc="DV")
        nb.wait("tensor", "DV")
        nb.op("tensor", "matmul", pb[7], ones_row, stat[0:1, 512:512 + TT],
              start=True, stop=True, waits=(bank_free[7],), inc="PE")
        nb.wait("vector", "PE")
        for k in range(8):
            nb.op("vector", "scalar_tensor_tensor", xn_t[:, k, :], x_t[:, k, :],
                  cst["rmsw"][:, k:k + 1], pb[7], OP.mult, OP.mult,
                  inc="DV" if k == 7 else None)
        dv_xn = nb.now("DV")
        bank_free[7] = ("DV", dv_xn)

        if tau == 0:
            nb.wait("vector", "DV", halo_dv)
            nb.op("vector", "tensor_copy", V_t[:, :, 0:3], vh_t, inc="DV")
        dv_vprev = nb.now("DV")

        def mm_group(wpan, m2, bi, rhs_t, nk, first_waits=()):
            fw = tuple(first_waits) + (bank_free[bi],)
            for k in range(nk):
                nb.op("tensor", "matmul", pb[bi],
                      wpan[:, k, m2 * 128:(m2 + 1) * 128], rhs_t[:, k, :],
                      start=(k == 0), stop=(k == nk - 1),
                      waits=fw if k == 0 else (),
                      inc="PE" if k == nk - 1 else None)
            return nb.now("PE")

        acV = [0] * 16
        for pn in range(8):
            wpan = wbuf(pn % 2, 8, 256)
            nb.load(wpan, winv[:, :, pn * 256:(pn + 1) * 256],
                    waits=(("PE", wb_free[pn % 2]),))
            nb.wait("tensor", "LD")
            nb.wait("tensor", "DV", dv_xn)
            if pn == 0:
                nb.wait("tensor", "DV", dv_vprev)
            for m2 in range(2):
                m = pn * 2 + m2
                pe_m = mm_group(wpan, m2, m % 4, xn_t, 8)
                nb.wait("scalar", "PE", pe_m)
                acV[m] = nb.op("scalar", "activation", V_t[:, m, 3:515],
                               pb[m % 4], AF.Identity,
                               bias=cst["binL"][:, m:m + 1], inc="AC")
                bank_free[m % 4] = ("AC", acV[m])
            wb_free[pn % 2] = nb.now("PE")

        ac_conv = {0: 0, 1: 0}
        for ci in range(16):
            par = ci % 2
            nb.wait("vector", "AC", max(acV[ci], ac_conv[par]))
            cw = cst["convw"]
            acc = hd(par)
            nb.op("vector", "tensor_scalar_mul", acc, V_t[:, ci, 0:TT],
                  cw[:, ci, 0:1])
            for tp in range(1, 4):
                nb.op("vector", "scalar_tensor_tensor", acc,
                      V_t[:, ci, tp:tp + TT], cw[:, ci, tp:tp + 1], acc,
                      OP.mult, OP.add)
            nb.op("vector", "nop", inc="DV")
            nb.wait("scalar", "DV", nb.now("DV"))
            ac_conv[par] = nb.op("scalar", "activation", u_t[:, ci, :], acc,
                                 AF.Silu, bias=cst["convb"][:, ci:ci + 1],
                                 inc="AC")
        ac_u = nb.now("AC")

        kst_free = 0
        for g4 in range(8):
            qname = "K0" if g4 < 4 else "K1"
            cg = g4 % 4
            for half in range(2):
                pn = 16 + g4 * 2 + half
                wpan = wbuf(pn % 2, 8, 256)
                nb.load(wpan, winv[:, :, pn * 256:(pn + 1) * 256],
                        waits=(("PE", wb_free[pn % 2]),))
                nb.wait("tensor", "LD")
                for m2 in range(2):
                    m = pn * 2 + m2
                    ci4 = half * 2 + m2
                    pe_m = mm_group(wpan, m2, m % 4, xn_t, 8)
                    nb.wait("scalar", "PE", max(pe_m, kst_free))
                    ac = nb.op("scalar", "activation", kst[:, ci4, :],
                               pb[m % 4], AF.Identity,
                               bias=cst["binL"][:, m:m + 1], inc="AC")
                    bank_free[m % 4] = ("AC", ac)
                wb_free[pn % 2] = nb.now("PE")
            kst_free = spill_group(tau, spblk[qname], cg,
                                   [kst[:, i, :] for i in range(4)],
                                   (("AC", nb.now("AC")),))

        for pn in range(8, 16):
            wpan = wbuf(pn % 2, 8, 256)
            nb.load(wpan, winv[:, :, pn * 256:(pn + 1) * 256],
                    waits=(("PE", wb_free[pn % 2]),))
            nb.wait("tensor", "LD")
            for m2 in range(2):
                m = pn * 2 + m2
                zc = m - 16
                par = zc % 2
                pe_m = mm_group(wpan, m2, m % 4, xn_t, 8)
                nb.wait("scalar", "PE", pe_m)
                nb.wait("scalar", "DV", g_store[par])
                ac_g = nb.op("scalar", "activation", g_t[:, par, :], pb[m % 4],
                             AF.Silu, bias=cst["binL"][:, m:m + 1], inc="AC")
                bank_free[m % 4] = ("AC", ac_g)
                nb.wait("vector", "AC", ac_g)
                nb.op("vector", "tensor_scalar_mul", gh_t[:, par, :],
                      g_t[:, par, :], cst["gnw"][:, zc:zc + 1], inc="DV")
                nb.store(sp["G"][zc * 128:(zc + 1) * 128, tau * TT:(tau + 1) * TT],
                         gh_t[:, par, :], waits=(("DV", nb.now("DV")),))
                nb.op("vector", "tensor_scalar_mul", g_t[:, par, :],
                      g_t[:, par, :], cst["gnb"][:, zc:zc + 1])
                nb.wait("vector", "AC", ac_u)
                g_store[par] = nb.op(
                    "vector", "scalar_tensor_tensor", g_t[:, par, :],
                    u_t[:, zc, :], cst["Dv"][:, zc:zc + 1], g_t[:, par, :],
                    OP.mult, OP.add, inc="DV")
                nb.store(sp["Hs"][zc * 128:(zc + 1) * 128, tau * TT:(tau + 1) * TT],
                         g_t[:, par, :], waits=(("DV", g_store[par]),))
            wb_free[pn % 2] = nb.now("PE")

        for mi in range(3):
            wpan = wbuf(mi % 2, 16, 128)
            nb.load(wpan, wsmv[:, :, mi * 128:(mi + 1) * 128],
                    waits=(("PE", wb_free[mi % 2]),))
            nb.wait("tensor", "LD")
            pe_m = mm_group(wpan, 0, mi % 2, u_t, 16,
                            first_waits=(("AC", ac_u),))
            wb_free[mi % 2] = pe_m
            nb.wait("scalar", "PE", pe_m)
            ac = nb.op("scalar", "activation", sm_t[:, mi, :], pb[mi % 2],
                       AF.Identity, bias=cst["bsmL"][:, mi:mi + 1], inc="AC")
            bank_free[mi % 2] = ("AC", ac)
        ac_sm = nb.now("AC")

        SELSv = cst["SELS"]

        def reloc(selcol, smi, bi, out_slot, func):
            nb.op("tensor", "matmul", pb[bi][0:32, :],
                  SELSv[:, selcol * 32:(selcol + 1) * 32], sm_t[:, smi, :],
                  start=True, stop=True,
                  waits=(("AC", ac_sm), bank_free[bi]), inc="PE")
            nb.wait("scalar", "PE", nb.now("PE"))
            ac = nb.op("scalar", "activation", hd(out_slot, 32),
                       pb[bi][0:32, :], func, inc="AC")
            bank_free[bi] = ("AC", ac)
            return ac

        reloc(0, 1, 2, 2, AF.Copy)
        reloc(1, 1, 3, 4, AF.Copy)
        reloc(4, 1, 2, 5, AF.Copy)
        reloc(0, 2, 3, 6, AF.Sigmoid)
        reloc(3, 2, 2, 7, AF.Sigmoid)
        reloc(2, 2, 3, 8, AF.Sigmoid)
        ac_rel = nb.now("AC")

        def softplus32(src_slot, dst_slot):
            nb.op("scalar", "activation", hd(0, 32), hd(src_slot, 32), AF.Abs)
            nb.op("scalar", "activation", hd(0, 32), hd(0, 32), AF.Exp, scale=-1.0)
            nb.op("scalar", "activation", hd(0, 32), hd(0, 32), AF.Ln, bias=1.0)
            nb.op("scalar", "activation", hd(1, 32), hd(src_slot, 32), AF.Relu,
                  inc="AC")
            nb.wait("vector", "AC")
            nb.op("vector", "tensor_add", hd(dst_slot, 32), hd(0, 32), hd(1, 32),
                  inc="DV")
            nb.wait("scalar", "DV", nb.now("DV"))

        nb.wait("vector", "AC", ac_rel)
        nb.wait("vector", "LD", ld_x)
        nb.op("vector", "tensor_add", hd(4, 32), hd(4, 32), pr_t[0:32, :],
              inc="DV")
        nb.wait("scalar", "DV", nb.now("DV"))
        softplus32(2, 2)
        softplus32(5, 5)
        nb.op("scalar", "activation", hd(0, 32), hd(4, 32), AF.Abs, inc="AC")
        nb.wait("vector", "AC")
        nb.op("vector", "tensor_add", hd(0, 32), hd(0, 32), hd(2, 32))
        nb.op("vector", "tensor_scalar_add", hd(0, 32), hd(0, 32), 1e-4)
        nb.op("vector", "reciprocal", hd(1, 32), hd(0, 32))
        nb.op("vector", "tensor_scalar_mul", hd(3, 32), hd(1, 32), spdtc)
        nb.op("vector", "tensor_add", hd(3, 32), hd(3, 32), hd(5, 32))
        nb.op("vector", "scalar_tensor_tensor", hd(0, 32), hd(3, 32), 0.5,
              hd(2, 32), OP.mult, OP.mult)
        nb.op("vector", "scalar_tensor_tensor", hd(1, 32), hd(3, 32), 0.5,
              hd(4, 32), OP.mult, OP.mult)
        nb.op("vector", "tensor_mul", hd(2, 32), hd(0, 32), hd(0, 32))
        nb.op("vector", "tensor_mul", hd(5, 32), hd(1, 32), hd(1, 32))
        nb.op("vector", "tensor_add", hd(2, 32), hd(2, 32), hd(5, 32))
        nb.op("vector", "tensor_scalar", hd(4, 32), hd(0, 32), 2.0, 1.0,
              OP.mult, OP.add)
        nb.op("vector", "tensor_add", hd(4, 32), hd(4, 32), hd(2, 32))
        nb.op("vector", "tensor_scalar", hd(5, 32), hd(0, 32), -2.0, 1.0,
              OP.mult, OP.add)
        nb.op("vector", "tensor_add", hd(5, 32), hd(5, 32), hd(2, 32))
        nb.op("vector", "reciprocal", hd(0, 32), hd(4, 32))
        nb.op("vector", "tensor_scalar", hd(9, 32), hd(2, 32), -1.0, 1.0,
              OP.mult, OP.add)
        nb.op("vector", "tensor_mul", hd(9, 32), hd(9, 32), hd(0, 32))
        nb.op("vector", "scalar_tensor_tensor", hd(10, 32), hd(1, 32), 2.0,
              hd(0, 32), OP.mult, OP.mult)
        nb.op("vector", "tensor_mul", hd(5, 32), hd(5, 32), hd(0, 32), inc="DV")
        nb.wait("scalar", "DV", nb.now("DV"))
        nb.op("scalar", "activation", hd(5, 32), hd(5, 32), AF.Ln, inc="AC")
        nb.wait("vector", "AC")
        nb.op("vector", "tensor_mul", hd(5, 32), hd(5, 32), hd(7, 32), inc="DV")
        nb.wait("scalar", "DV", nb.now("DV"))
        nb.op("scalar", "activation", hd(5, 32), hd(5, 32), AF.Exp, inc="AC")
        nb.wait("vector", "AC")
        nb.op("vector", "tensor_scalar", hd(5, 32), hd(5, 32), -1.0, 1.0,
              OP.mult, OP.add)
        nb.op("vector", "tensor_scalar_max", hd(5, 32), hd(5, 32), 1e-6,
              inc="DV")
        nb.wait("scalar", "DV", nb.now("DV"))
        nb.op("scalar", "activation", hd(5, 32), hd(5, 32), AF.Sqrt, inc="AC")
        nb.wait("vector", "AC")
        nb.op("vector", "tensor_mul", hd(11, 32), hd(5, 32), hd(8, 32),
              inc="DV")
        dv_head = nb.now("DV")

        for blk in range(4):
            bi = 4 + blk % 2
            nb.op("tensor", "transpose", pb[bi][:, 0:128],
                  sm_t[:, 0, blk * 128:(blk + 1) * 128], ident,
                  waits=(("AC", ac_sm), ("DV", dv_head), bank_free[bi]))
            for qi, slot in enumerate((9, 10, 6, 11)):
                nb.op("tensor", "transpose",
                      pb[bi][0:128, 128 + qi * 32:128 + (qi + 1) * 32],
                      hd(slot, 32)[:, blk * 128:(blk + 1) * 128],
                      ident[0:32, 0:32], inc="PE" if qi == 3 else None)
            nb.wait("scalar", "PE", nb.now("PE"))
            ac = nb.op("scalar", "activation", ph_t[:, blk, :],
                       pb[bi][:, 0:256], AF.Copy, inc="AC")
            bank_free[bi] = ("AC", ac)
        nb.store(sphblk[tau], ph_t, waits=(("AC", nb.now("AC")),))

        for g4 in range(8):
            qname = "Q0" if g4 < 4 else "Q1"
            cg = g4 % 4
            for half in range(2):
                pn = g4 * 2 + half
                wpanq = wbuf(pn % 2, 16, 256)
                nb.load(wpanq, wqv[:, :, pn * 256:(pn + 1) * 256],
                        waits=(("PE", wb_free[pn % 2]),))
                nb.wait("tensor", "LD")
                for m2 in range(2):
                    ci4 = half * 2 + m2
                    bi = ci4 % 4
                    pe_m = mm_group(wpanq, m2, bi, u_t, 16,
                                    first_waits=(("AC", ac_u),))
                    nb.wait("scalar", "PE", max(pe_m, kst_free))
                    ac = nb.op("scalar", "activation", kst[:, ci4, :], pb[bi],
                               AF.Copy, inc="AC")
                    bank_free[bi] = ("AC", ac)
                wb_free[pn % 2] = nb.now("PE")
            kst_free = spill_group(tau, spblk[qname], cg,
                                   [kst[:, i, :] for i in range(4)],
                                   (("AC", nb.now("AC")),))

        for cg in range(4):
            spill_group(tau, spblk["U"], cg,
                        [u_t[:, cg * 4 + i, :] for i in range(4)],
                        (("AC", ac_u),))

        if tau < NT - 1:
            nb.op("vector", "tensor_copy", V_t[:, :, 0:3], V_t[:, :, 512:515],
                  waits=(("AC", ac_u),), inc="DV")
        if _STOP_AFTER == f"t{tau}":
            break

    nb.barrier()
    if _STOP_AFTER == "p1" or (_STOP_AFTER or "").startswith("t"):
        _finish(nb)
        return nb, OUT

    o_S0p, o_S1p = 0, 6144
    o_tA, o_tB = 12288, 18432
    o_k0, o_k1, o_bk0, o_bk1 = 24576, 26624, 28672, 30720
    o_v = 32768
    o_stp = 34816
    o_php = 47104
    assert o_php + 512 <= POOL_N

    S0p = P(o_S0p, 3, 2048); S1p = P(o_S1p, 3, 2048)
    S0p4 = P(o_S0p, 3, 32, 64); S1p4 = P(o_S1p, 3, 32, 64)
    tA = P(o_tA, 3, 2048); tB = P(o_tB, 3, 2048)
    tA4 = P(o_tA, 3, 32, 64); tB4 = P(o_tB, 3, 32, 64)
    k0a = P(o_k0, 2048); k1a = P(o_k1, 2048)
    bk0a = P(o_bk0, 2048); bk1a = P(o_bk1, 2048)
    va = P(o_v, 2048)
    stp = P(o_stp, 6, 2048)
    php = P(o_php, 2, 256)

    def b3(t):
        return t.unsqueeze(1).broadcast_to([128, 3, 2048])

    def phb(src, par, qi):
        return src[:, par, qi * 32:(qi + 1) * 32].unsqueeze(2).broadcast_to(
            [128, 32, 64])

    def phb3(par, qi):
        return php[:, par, qi * 32:(qi + 1) * 32].unsqueeze(1).unsqueeze(3)\
            .broadcast_to([128, 3, 32, 64])

    def vw(t):
        return t.rearrange("p (h d) -> p h d", h=32)

    nb.op("vector", "memset", S0p, 0.0)
    nb.op("vector", "memset", S1p, 0.0)
    nb.op("vector", "memset", S0p[:, 0, :], 1.0)
    nb.op("vector", "memset", S1p[:, 1, :], 1.0, inc="DV")

    dv_step = {}
    for j in range(CH):
        par = j % 2
        w_free = (("DV", dv_step[j - 2]),) if j >= 2 else ()
        nb.load(stp[:, 0 + par, :], spj["K0"][:, j, :], waits=w_free)
        nb.load(stp[:, 2 + par, :], spj["K1"][:, j, :])
        nb.load(stp[:, 4 + par, :], spj["U"][:, j, :])
        nb.load(php[:, par, :], sphj[:, j, :])
        ldj = nb.now("LD")
        nb.wait("vector", "LD", ldj)
        nb.op("vector", "tensor_mul", vw(k0a), vw(stp[:, 0 + par, :]), phb(php, par, 0))
        nb.op("vector", "tensor_mul", vw(k1a), vw(stp[:, 2 + par, :]), phb(php, par, 1))
        nb.op("vector", "tensor_mul", vw(va), vw(stp[:, 4 + par, :]), phb(php, par, 7))
        nb.op("vector", "tensor_mul", vw(bk0a), vw(k0a), phb(php, par, 6))
        nb.op("vector", "tensor_mul", vw(bk1a), vw(k1a), phb(php, par, 6))
        nb.op("vector", "tensor_mul", tA4, S1p4, phb3(par, 5))
        nb.op("vector", "tensor_mul", tB4, S0p4, phb3(par, 5))
        nb.op("vector", "tensor_mul", S0p4, S0p4, phb3(par, 4))
        nb.op("vector", "tensor_add", S0p, S0p, tA)
        nb.op("vector", "tensor_mul", S1p4, S1p4, phb3(par, 4))
        nb.op("vector", "tensor_sub", S1p, S1p, tB)
        nb.op("vector", "tensor_mul", tA, S0p, b3(k0a))
        nb.op("vector", "tensor_mul", tB, S1p, b3(k1a))
        nb.op("vector", "tensor_add", tA, tA, tB)
        nb.op("vector", "tensor_scalar_mul", tB[:, 0:2, :], tA[:, 0:2, :], -1.0)
        nb.op("vector", "tensor_sub", tB[:, 2, :], va, tA[:, 2, :])
        nb.op("vector", "tensor_mul", tA, tB, b3(bk0a))
        nb.op("vector", "tensor_add", S0p, S0p, tA)
        nb.op("vector", "tensor_mul", tA, tB, b3(bk1a))
        nb.op("vector", "tensor_add", S1p, S1p, tA, inc="DV")
        dv_step[j] = nb.now("DV")

    nb.barrier()
    if _STOP_AFTER == "pA":
        for i, (src, r) in enumerate([(S0p, 0), (S0p, 1), (S0p, 2),
                                      (S1p, 0), (S1p, 1), (S1p, 2)]):
            nb.store(spj["Y"][:, i, :], src[:, r, :])
        nb.barrier()
        _finish(nb)
        return nb, OUT

    Pc = [P(o_tA + i * 2048, 16, 128) for i in range(6)]
    ct = P(o_k0, 8, 16, 16)
    srcs = [(S0p, 0), (S0p, 1), (S0p, 2), (S1p, 0), (S1p, 1), (S1p, 2)]
    for qi, (src, r) in enumerate(srcs):
        for cg in range(4):
            bi = 4 + cg % 2
            for ci in range(4):
                cc = cg * 4 + ci
                nb.op("tensor", "transpose", pb[bi][:, ci * 128:(ci + 1) * 128],
                      src[:, r, cc * 128:(cc + 1) * 128], ident,
                      waits=(bank_free[bi],) if ci == 0 else (),
                      inc="PE" if ci == 3 else None)
            nb.wait("scalar", "PE", nb.now("PE"))
            ac = nb.op("scalar", "activation",
                       Pc[qi][:, cg * 4:(cg + 1) * 4, :],
                       pb[bi].rearrange("p (a b) -> p a b", a=4), AF.Copy,
                       inc="AC")
            bank_free[bi] = ("AC", ac)
    nb.wait("vector", "AC", nb.now("AC"))

    def emit_compose(cur, prv, tmps):
        C00, C01, Cq0, C10, C11, Cq1 = cur
        Pp00, Pp01, Pq0, Pp10, Pp11, Pq1 = prv
        t00, t01, t10, t11, tq0, tq1, tx = tmps
        nb.op("vector", "tensor_mul", t00, C00, Pp00)
        nb.op("vector", "tensor_mul", tx, C01, Pp10)
        nb.op("vector", "tensor_add", t00, t00, tx)
        nb.op("vector", "tensor_mul", t01, C00, Pp01)
        nb.op("vector", "tensor_mul", tx, C01, Pp11)
        nb.op("vector", "tensor_add", t01, t01, tx)
        nb.op("vector", "tensor_mul", t10, C10, Pp00)
        nb.op("vector", "tensor_mul", tx, C11, Pp10)
        nb.op("vector", "tensor_add", t10, t10, tx)
        nb.op("vector", "tensor_mul", t11, C10, Pp01)
        nb.op("vector", "tensor_mul", tx, C11, Pp11)
        nb.op("vector", "tensor_add", t11, t11, tx)
        nb.op("vector", "tensor_mul", tq0, C00, Pq0)
        nb.op("vector", "tensor_mul", tx, C01, Pq1)
        nb.op("vector", "tensor_add", tq0, tq0, tx)
        nb.op("vector", "tensor_add", tq0, tq0, Cq0)
        nb.op("vector", "tensor_mul", tq1, C10, Pq0)
        nb.op("vector", "tensor_mul", tx, C11, Pq1)
        nb.op("vector", "tensor_add", tq1, tq1, tx)
        nb.op("vector", "tensor_add", tq1, tq1, Cq1)
        for d, t_ in zip(cur, (t00, t01, tq0, t10, t11, tq1)):
            nb.op("vector", "tensor_copy", d, t_)

    tmps16 = [ct[:, i, :, :] for i in range(7)]
    for s_ in range(1, 8):
        emit_compose([Pc[i][:, :, s_::8] for i in range(6)],
                     [Pc[i][:, :, s_ - 1::8] for i in range(6)], tmps16)
    tmps1 = [ct[:, i, :, 0:1] for i in range(7)]
    for g_ in range(1, 16):
        emit_compose([Pc[i][:, :, g_ * 8 + 7:g_ * 8 + 8] for i in range(6)],
                     [Pc[i][:, :, g_ * 8 - 1:g_ * 8] for i in range(6)], tmps1)

    packb = P(o_k1, 6, 16)
    for i in range(6):
        nb.op("vector", "tensor_copy", packb[:, i, :],
              Pc[i][:, :, 127:128].rearrange("p a b -> p (a b)"),
              inc="DV" if i == 5 else None)
    nb.store(cc1i, packb.rearrange("p a b -> p (a b)"),
             waits=(("DV", nb.now("DV")),))
    nb.op("gpsimd", "collective_compute", "AllGather", OP.bypass,
          [list(range(NC))], [cc1i], [cc1o],
          waits=(("ST", nb.now("ST")),), inc="CC")
    agT = P(o_bk0, 8, 6, 16)
    nb.load(agT.rearrange("p a b c -> p a (b c)"),
            cc1o.rearrange("(r p) c -> p r c", p=128),
            waits=(("CC", nb.now("CC")),))
    ld_ag = nb.now("LD")

    Z = P(o_v, 4, 16)
    Zs = P(o_v + 1024, 2, 16)
    ctmp = P(o_v + 1536, 16)
    nb.op("vector", "memset", Zs, 0.0)
    nb.wait("vector", "LD", ld_ag)
    for b_ in range(2):
        nb.op("vector", "memset", Z[:, 0, :], 0.0)
        nb.op("vector", "memset", Z[:, 1, :], 0.0)
        cur = 0
        for i_ in range(4):
            r = 4 * b_ + i_
            if i_ > 0:
                nb.op("vector", "scalar_tensor_tensor", Zs[:, 0, :],
                      Z[:, cur, :], cst["sel8"][:, r:r + 1], Zs[:, 0, :],
                      OP.mult, OP.add)
                nb.op("vector", "scalar_tensor_tensor", Zs[:, 1, :],
                      Z[:, cur + 1, :], cst["sel8"][:, r:r + 1], Zs[:, 1, :],
                      OP.mult, OP.add)
            if i_ < 3:
                nxt = 2 - cur
                ag = agT[:, r, :, :]
                nb.op("vector", "tensor_mul", Z[:, nxt, :], ag[:, 0, :], Z[:, cur, :])
                nb.op("vector", "tensor_mul", ctmp, ag[:, 1, :], Z[:, cur + 1, :])
                nb.op("vector", "tensor_add", Z[:, nxt, :], Z[:, nxt, :], ctmp)
                nb.op("vector", "tensor_add", Z[:, nxt, :], Z[:, nxt, :], ag[:, 2, :])
                nb.op("vector", "tensor_mul", Z[:, nxt + 1, :], ag[:, 3, :], Z[:, cur, :])
                nb.op("vector", "tensor_mul", ctmp, ag[:, 4, :], Z[:, cur + 1, :])
                nb.op("vector", "tensor_add", Z[:, nxt + 1, :], Z[:, nxt + 1, :], ctmp)
                nb.op("vector", "tensor_add", Z[:, nxt + 1, :], Z[:, nxt + 1, :], ag[:, 5, :])
                cur = nxt

    GPs = [Pc[i][:, :, 7:127:8] for i in range(6)]
    Tg = P(o_stp, 2, 16, 16)
    ct2 = P(o_stp + 1024, 16, 15)
    Sb0 = Zs[:, 0, :].unsqueeze(2).broadcast_to([128, 16, 15])
    Sb1 = Zs[:, 1, :].unsqueeze(2).broadcast_to([128, 16, 15])
    nb.op("vector", "tensor_copy", Tg[:, 0, :, 0:1], Zs[:, 0, :].unsqueeze(2))
    nb.op("vector", "tensor_copy", Tg[:, 1, :, 0:1], Zs[:, 1, :].unsqueeze(2))
    nb.op("vector", "tensor_mul", Tg[:, 0, :, 1:16], GPs[0], Sb0)
    nb.op("vector", "tensor_mul", ct2, GPs[1], Sb1)
    nb.op("vector", "tensor_add", Tg[:, 0, :, 1:16], Tg[:, 0, :, 1:16], ct2)
    nb.op("vector", "tensor_add", Tg[:, 0, :, 1:16], Tg[:, 0, :, 1:16], GPs[2])
    nb.op("vector", "tensor_mul", Tg[:, 1, :, 1:16], GPs[3], Sb0)
    nb.op("vector", "tensor_mul", ct2, GPs[4], Sb1)
    nb.op("vector", "tensor_add", Tg[:, 1, :, 1:16], Tg[:, 1, :, 1:16], ct2)
    nb.op("vector", "tensor_add", Tg[:, 1, :, 1:16], Tg[:, 1, :, 1:16], GPs[5])
    Sin0 = P(o_stp + 2048, 16, 128)
    Sin1 = P(o_stp + 4096, 16, 128)
    ct3 = P(o_stp + 6144, 16, 16)
    T0 = Tg[:, 0, :, :]
    T1 = Tg[:, 1, :, :]
    nb.op("vector", "tensor_copy", Sin0[:, :, 0::8], T0)
    nb.op("vector", "tensor_copy", Sin1[:, :, 0::8], T1)
    for s_ in range(1, 8):
        WG = [Pc[i][:, :, s_ - 1::8] for i in range(6)]
        nb.op("vector", "tensor_mul", Sin0[:, :, s_::8], WG[0], T0)
        nb.op("vector", "tensor_mul", ct3, WG[1], T1)
        nb.op("vector", "tensor_add", Sin0[:, :, s_::8], Sin0[:, :, s_::8], ct3)
        nb.op("vector", "tensor_add", Sin0[:, :, s_::8], Sin0[:, :, s_::8], WG[2])
        nb.op("vector", "tensor_mul", Sin1[:, :, s_::8], WG[3], T0)
        nb.op("vector", "tensor_mul", ct3, WG[4], T1)
        nb.op("vector", "tensor_add", Sin1[:, :, s_::8], Sin1[:, :, s_::8], ct3)
        nb.op("vector", "tensor_add", Sin1[:, :, s_::8], Sin1[:, :, s_::8], WG[5])
    nb.op("vector", "nop", inc="DV")
    nb.barrier()

    S0s = P(0, 2048)
    S1s = P(2048, 2048)
    for dst, srct in ((S0s, Sin0), (S1s, Sin1)):
        for cg in range(4):
            bi = 4 + cg % 2
            for ci in range(4):
                cc = cg * 4 + ci
                nb.op("tensor", "transpose", pb[bi][:, ci * 128:(ci + 1) * 128],
                      srct[:, cc, :], ident,
                      waits=(bank_free[bi],) if ci == 0 else (),
                      inc="PE" if ci == 3 else None)
            nb.wait("scalar", "PE", nb.now("PE"))
            ac = nb.op("scalar", "activation",
                       dst.rearrange("p (a b) -> p a b", a=16)
                       [:, cg * 4:(cg + 1) * 4, :],
                       pb[bi].rearrange("p (a b) -> p a b", a=4), AF.Copy,
                       inc="AC")
            bank_free[bi] = ("AC", ac)
    nb.barrier()

    o_ysum, o_y2 = 4096, 6144
    o_k0c, o_k1c, o_vc, o_q0c, o_q1c = 8192, 10240, 12288, 14336, 16384
    o_t1, o_t2 = 18432, 20480
    o_stpC = 22528
    o_ybuf = 43008
    o_phpC = 47104
    ysum = P(o_ysum, 2048); y2sum = P(o_y2, 2048)
    k0c = P(o_k0c, 2048); k1c = P(o_k1c, 2048); vc = P(o_vc, 2048)
    q0c = P(o_q0c, 2048); q1c = P(o_q1c, 2048)
    t1 = P(o_t1, 2048); t2 = P(o_t2, 2048)
    stpC = P(o_stpC, 10, 2048)
    ybuf = P(o_ybuf, 2, 2048)
    phpC = P(o_phpC, 2, 256)

    nb.op("vector", "memset", ysum, 0.0)
    nb.op("vector", "memset", y2sum, 0.0, inc="DV")
    dv_stepC = {}
    st_y = {0: 0, 1: 0}
    for j in range(CH):
        par = j % 2
        w_free = (("DV", dv_stepC[j - 2]),) if j >= 2 else ()
        nb.load(stpC[:, 0 + par, :], spj["K0"][:, j, :], waits=w_free)
        nb.load(stpC[:, 2 + par, :], spj["K1"][:, j, :])
        nb.load(stpC[:, 4 + par, :], spj["U"][:, j, :])
        nb.load(stpC[:, 6 + par, :], spj["Q0"][:, j, :])
        nb.load(stpC[:, 8 + par, :], spj["Q1"][:, j, :])
        nb.load(phpC[:, par, :], sphj[:, j, :])
        ldj = nb.now("LD")
        nb.wait("vector", "LD", ldj)
        nb.op("vector", "tensor_mul", vw(k0c), vw(stpC[:, 0 + par, :]), phb(phpC, par, 0))
        nb.op("vector", "tensor_mul", vw(k1c), vw(stpC[:, 2 + par, :]), phb(phpC, par, 1))
        nb.op("vector", "tensor_mul", vw(vc), vw(stpC[:, 4 + par, :]), phb(phpC, par, 7))
        nb.op("vector", "tensor_mul", vw(t1), vw(S1s), phb(phpC, par, 5))
        nb.op("vector", "tensor_mul", vw(t2), vw(S0s), phb(phpC, par, 5))
        nb.op("vector", "tensor_mul", vw(S0s), vw(S0s), phb(phpC, par, 4))
        nb.op("vector", "tensor_add", S0s, S0s, t1)
        nb.op("vector", "tensor_mul", vw(S1s), vw(S1s), phb(phpC, par, 4))
        nb.op("vector", "tensor_sub", S1s, S1s, t2)
        nb.op("vector", "tensor_mul", t1, k0c, S0s)
        nb.op("vector", "tensor_mul", t2, k1c, S1s)
        nb.op("vector", "tensor_add", t1, t1, t2)
        nb.op("vector", "tensor_sub", t1, vc, t1)
        nb.op("vector", "tensor_mul", vw(t1), vw(t1), phb(phpC, par, 6))
        nb.op("vector", "tensor_mul", t2, t1, k0c)
        nb.op("vector", "tensor_add", S0s, S0s, t2)
        nb.op("vector", "tensor_mul", t2, t1, k1c)
        nb.op("vector", "tensor_add", S1s, S1s, t2)
        nb.op("vector", "tensor_mul", vw(q0c), vw(stpC[:, 6 + par, :]), phb(phpC, par, 2))
        nb.op("vector", "tensor_mul", vw(q1c), vw(stpC[:, 8 + par, :]), phb(phpC, par, 3))
        nb.wait("vector", "ST", st_y[par])
        nb.op("vector", "tensor_mul", ybuf[:, par, :], q0c, S0s)
        nb.op("vector", "tensor_mul", t2, q1c, S1s)
        nb.op("vector", "tensor_add", ybuf[:, par, :], ybuf[:, par, :], t2)
        nb.op("vector", "tensor_add", ysum, ysum, ybuf[:, par, :])
        nb.op("vector", "tensor_mul", t2, ybuf[:, par, :], ybuf[:, par, :])
        nb.op("vector", "tensor_add", y2sum, y2sum, t2, inc="DV")
        dv_stepC[j] = nb.now("DV")
        st_y[par] = nb.store(spj["Y"][:, j, :], ybuf[:, par, :],
                             waits=(("DV", dv_stepC[j]),))
    nb.barrier()
    if _STOP_AFTER == "pC":
        _finish(nb)
        return nb, OUT

    for q_ in range(4):
        nb.op("tensor", "matmul", pb[6][0:1, :], ones_col,
              ysum[:, q_ * 512:(q_ + 1) * 512], start=True, stop=True,
              waits=(bank_free[6],), inc="PE")
        nb.wait("scalar", "PE", nb.now("PE"))
        ac = nb.op("scalar", "activation", stat[0:1, q_ * 512:(q_ + 1) * 512],
                   pb[6][0:1, :], AF.Copy, inc="AC")
        bank_free[6] = ("AC", ac)
        nb.op("tensor", "matmul", pb[7][0:1, :], ones_col,
              y2sum[:, q_ * 512:(q_ + 1) * 512], start=True, stop=True,
              waits=(bank_free[7],), inc="PE")
        nb.wait("scalar", "PE", nb.now("PE"))
        ac = nb.op("scalar", "activation",
                   stat[0:1, 2048 + q_ * 512:2048 + (q_ + 1) * 512],
                   pb[7][0:1, :], AF.Copy, inc="AC")
        bank_free[7] = ("AC", ac)
    nb.wait("vector", "AC", nb.now("AC"))
    hsum = P(o_t1, 64)
    nb.op("vector", "tensor_reduce", hsum[0:1, 0:32].unsqueeze(2),
          stat[0:1, 0:2048].rearrange("p (h d) -> p h d", h=32), AX, OP.add)
    nb.op("vector", "tensor_reduce", hsum[0:1, 32:64].unsqueeze(2),
          stat[0:1, 2048:4096].rearrange("p (h d) -> p h d", h=32), AX, OP.add,
          inc="DV")
    nb.store(cc2i, hsum[0:1, :], waits=(("DV", nb.now("DV")),))
    nb.op("gpsimd", "collective_compute", "AllGather", OP.bypass,
          [list(range(NC))], [cc2i], [cc2o],
          waits=(("ST", nb.now("ST")),), inc="CC")
    nb.load(cc2s, cc2o, waits=(("CC", nb.now("CC")),))
    nb.wait("tensor", "LD")
    nb.op("tensor", "matmul", pb[6][0:1, 0:64], cst["bmask"], cc2s,
          start=True, stop=True, waits=(bank_free[6],), inc="PE")
    nb.wait("vector", "PE")
    NTOT = float(HD * S)
    nb.op("vector", "tensor_scalar_mul", stat[0:1, 0:64], pb[6][0:1, 0:64],
          1.0 / NTOT)
    nb.op("vector", "tensor_mul", stat[0:1, 64:96], stat[0:1, 0:32],
          stat[0:1, 0:32])
    nb.op("vector", "tensor_sub", stat[0:1, 32:64], stat[0:1, 32:64],
          stat[0:1, 64:96], inc="DV")
    bank_free[6] = ("DV", nb.now("DV"))
    nb.wait("scalar", "DV", nb.now("DV"))
    nb.op("scalar", "activation", stat[0:1, 32:64], stat[0:1, 32:64],
          AF.Sqrt, bias=1e-5, inc="AC")
    nb.wait("vector", "AC")
    nb.op("vector", "reciprocal", stat[0:1, 32:64], stat[0:1, 32:64], inc="DV")
    for hi_ in range(2):
        for comp_ in range(2):
            dst = stat[0:1, 128 + hi_ * 32 + comp_ * 16:128 + hi_ * 32 + comp_ * 16 + 16]
            src_ = stat[0:1, comp_ * 32 + hi_:comp_ * 32 + hi_ + 31 + 1]\
                .rearrange("p (cc two) -> p cc two", two=2)[:, :, 0:1]\
                .rearrange("p cc two -> p (cc two)")
            nb.op("vector", "tensor_copy", dst, src_)
    nb.op("vector", "nop", inc="DV")
    nb.wait("tensor", "DV", nb.now("DV"))
    nb.op("tensor", "transpose", pb[7][0:64, 0:1], stat[0:1, 128:192],
          ident[0:1, 0:1], waits=(bank_free[7],), inc="PE")
    nb.wait("scalar", "PE", nb.now("PE"))
    ac = nb.op("scalar", "activation", mr64, pb[7][0:64, 0:1], AF.Copy, inc="AC")
    bank_free[7] = ("AC", ac)
    nb.store(sp["MR"], mr64, waits=(("AC", nb.now("AC")),))
    st_mr = nb.now("ST")
    for hi in range(2):
        for comp in range(2):
            base = hi * 32 + comp * 16
            src = sp["MR"][base:base + 16, :].transpose([1, 0])\
                .broadcast_to([64, 16])
            nb.gdma(murc[hi * 64:(hi + 1) * 64, comp, :], src,
                    waits=(("ST", st_mr),))
    nb.barrier()

    o_ysc, o_yg = 0, 8192
    o_gh3, o_x3 = 16384, 18432
    o_x23, o_wo = 22528, 26624
    o_hall = POOL_N - 16384
    assert o_wo + 4096 <= o_hall
    ysc = P(o_ysc, 4, 2048)
    yg = P(o_yg, 16, 512)
    gh3 = P(o_gh3, 4, 512)
    x3 = P(o_x3, 8, 512)
    x23 = P(o_x23, 8, 512)
    h_all = P(o_hall, 8, 2048)
    sq3a = P(o_ysc, 512)

    def wobuf(i):
        return P(o_wo + i * 2048, 16, 128)

    def stats_fm(src_t, sq_ap):
        nb.wait("tensor", "DV", nb.now("DV"))
        for k in range(8):
            nb.op("tensor", "matmul", pb[6][0:1, :], ones_col, src_t[:, k, :],
                  start=(k == 0), stop=(k == 7),
                  waits=(bank_free[6],) if k == 0 else (),
                  inc="PE" if k == 7 else None)
        nb.wait("scalar", "PE", nb.now("PE"))
        ac = nb.op("scalar", "activation", stat[0:1, 0:512], pb[6][0:1, :],
                   AF.Copy, inc="AC")
        bank_free[6] = ("AC", ac)
        pe_prev = 0
        for k in range(8):
            nb.wait("vector", "PE", pe_prev)
            nb.op("vector", "tensor_mul", sq_ap, src_t[:, k, :], src_t[:, k, :],
                  inc="DV")
            nb.wait("tensor", "DV", nb.now("DV"))
            nb.op("tensor", "matmul", pb[7][0:1, :], ones_col, sq_ap,
                  start=(k == 0), stop=(k == 7),
                  waits=(bank_free[7],) if k == 0 else (), inc="PE")
            pe_prev = nb.now("PE")
        nb.wait("scalar", "PE", nb.now("PE"))
        ac = nb.op("scalar", "activation", stat[0:1, 512:1024], pb[7][0:1, :],
                   AF.Copy, inc="AC")
        bank_free[7] = ("AC", ac)
        nb.wait("vector", "AC", nb.now("AC"))
        nb.op("vector", "tensor_scalar_mul", stat[0:1, 1024:1536],
              stat[0:1, 0:512], 1.0 / DM)
        nb.op("vector", "tensor_scalar_mul", stat[0:1, 512:1024],
              stat[0:1, 512:1024], 1.0 / DM)
        nb.op("vector", "tensor_mul", stat[0:1, 1536:2048],
              stat[0:1, 1024:1536], stat[0:1, 1024:1536])
        nb.op("vector", "tensor_sub", stat[0:1, 512:1024],
              stat[0:1, 512:1024], stat[0:1, 1536:2048], inc="DV")
        nb.wait("scalar", "DV", nb.now("DV"))
        nb.op("scalar", "activation", stat[0:1, 512:1024], stat[0:1, 512:1024],
              AF.Sqrt, bias=1e-5, inc="AC")
        nb.wait("vector", "AC")
        nb.op("vector", "reciprocal", stat[0:1, 1536:2048],
              stat[0:1, 512:1024], inc="DV")
        nb.wait("tensor", "DV", nb.now("DV"))
        nb.op("tensor", "matmul", pb[6], ones_row, stat[0:1, 1024:1536],
              start=True, stop=True, waits=(bank_free[6],))
        nb.op("tensor", "matmul", pb[7], ones_row, stat[0:1, 1536:2048],
              start=True, stop=True, waits=(bank_free[7],), inc="PE")
        nb.wait("vector", "PE", nb.now("PE"))

    for tau in range(NT):
        dv_pre = nb.now("DV")
        for blk in range(4):
            nb.load(ysc[:, blk, :],
                    sp["Y"][tau * 512 + blk * 128:tau * 512 + (blk + 1) * 128, :],
                    waits=(("DV", dv_pre), ("PE", nb.now("PE"))) if blk == 0 else ())
        nb.load(x3, xv[:, :, 3 + tau * TT:3 + (tau + 1) * TT])
        ld3 = nb.now("LD")
        for blk in range(4):
            for cg in range(4):
                bi = 4 + cg % 2
                for ci in range(4):
                    cc = cg * 4 + ci
                    nb.op("tensor", "transpose",
                          pb[bi][:, ci * 128:(ci + 1) * 128],
                          ysc[:, blk, cc * 128:(cc + 1) * 128], ident,
                          waits=(bank_free[bi], ("LD", ld3)) if ci == 0 else (),
                          inc="PE" if ci == 3 else None)
                nb.wait("scalar", "PE", nb.now("PE"))
                ac = nb.op("scalar", "activation",
                           yg[:, cg * 4:(cg + 1) * 4, blk * 128:(blk + 1) * 128],
                           pb[bi].rearrange("p (a b) -> p a b", a=4), AF.Copy,
                           inc="AC")
                bank_free[bi] = ("AC", ac)
        ac_yg = nb.now("AC")
        gst = {0: 0, 1: 0}
        for cc in range(16):
            par = cc % 2
            nb.load(gh3[:, par, :],
                    sp["G"][cc * 128:(cc + 1) * 128, tau * TT:(tau + 1) * TT],
                    waits=(("DV", gst[par]),))
            nb.load(gh3[:, 2 + par, :],
                    sp["Hs"][cc * 128:(cc + 1) * 128, tau * TT:(tau + 1) * TT])
            ldgh = nb.now("LD")
            nb.wait("vector", "AC", ac_yg)
            nb.wait("vector", "LD", ldgh)
            nb.op("vector", "tensor_scalar", yg[:, cc, :], yg[:, cc, :],
                  murc[:, 0, cc:cc + 1], murc[:, 1, cc:cc + 1], OP.subtract, OP.mult)
            nb.op("vector", "tensor_mul", yg[:, cc, :], yg[:, cc, :],
                  gh3[:, par, :])
            gst[par] = nb.op("vector", "tensor_add", yg[:, cc, :],
                             yg[:, cc, :], gh3[:, 2 + par, :], inc="DV")
        dv_yg = nb.now("DV")
        for m in range(8):
            wo = wobuf(m % 2)
            nb.load(wo, woutv[:, :, m * 128:(m + 1) * 128],
                    waits=(("PE", wb_free[m % 2]),))
            nb.wait("tensor", "LD")
            nb.wait("tensor", "DV", dv_yg)
            fw = (bank_free[m % 2],)
            for k in range(16):
                nb.op("tensor", "matmul", pb[m % 2], wo[:, k, :], yg[:, k, :],
                      start=(k == 0), stop=(k == 15),
                      waits=fw if k == 0 else (),
                      inc="PE" if k == 15 else None)
            wb_free[m % 2] = nb.now("PE")
            nb.wait("vector", "PE", nb.now("PE"))
            nb.wait("vector", "LD", ld3)
            dv = nb.op("vector", "scalar_tensor_tensor", x3[:, m, :],
                       x3[:, m, :], 2.0, pb[m % 2], OP.mult, OP.add, inc="DV")
            bank_free[m % 2] = ("DV", dv)
        stats_fm(x3, sq3a)
        for k in range(8):
            nb.op("vector", "tensor_sub", x23[:, k, :], x3[:, k, :], pb[6])
            nb.op("vector", "tensor_mul", x23[:, k, :], x23[:, k, :], pb[7],
                  inc="DV" if k == 7 else None)
        dv_x2 = nb.now("DV")
        bank_free[6] = ("DV", dv_x2)
        bank_free[7] = ("DV", dv_x2)
        for k in range(8):
            nb.store(x2v[:, k, tau * TT:(tau + 1) * TT], x23[:, k, :],
                     waits=(("DV", dv_x2),) if k == 0 else ())
        stats_fm(x23, sq3a)
        for k in range(8):
            nb.op("vector", "tensor_sub", sq3a, x23[:, k, :], pb[6])
            nb.op("vector", "tensor_mul", sq3a, sq3a, pb[7])
            nb.op("vector", "tensor_scalar",
                  h_all[:, k, tau * TT:(tau + 1) * TT], sq3a,
                  cst["lng"][:, k:k + 1], cst["lnb"][:, k:k + 1],
                  OP.mult, OP.add, inc="DV" if k == 7 else None)
        bank_free[6] = ("DV", nb.now("DV"))
        bank_free[7] = ("DV", nb.now("DV"))
    nb.barrier()
    if _DEBUG_SPILL and globals().get("_DUMP_H", False):
        for k in range(8):
            nb.store(sp["Y"][k * 128:(k + 1) * 128, :], h_all[:, k, :])
        nb.barrier()
    if _STOP_AFTER == "p3a":
        _finish(nb)
        return nb, OUT

    o_fq = 0
    o_w1d = 2048
    o_sc1 = 3072
    o_s1o = 5120
    fqb = [P(o_fq + i * 1024, 8, 128, i32=True) for i in range(2)]
    w1d = P(o_w1d, 8, 128)
    sc1b = [P(o_sc1 + i * 1024, 8, 128) for i in range(2)]
    s1o = [P(o_s1o + i * 512, 512) for i in range(2)]
    fq_free = {0: 0, 1: 0}
    s1st = {0: 0, 1: 0}
    for m in range(32):
        par = m % 2
        nb.load(fqb[par], fc1qv[:, :, m * 128:(m + 1) * 128],
                waits=(("DV", fq_free[par]),))
        nb.gdma(sc1b[par],
                ins["fc1s"][:, m * 128:(m + 1) * 128].unsqueeze(0)
                .broadcast_to([128, 8, 128]),
                waits=(("DV", fq_free[par]),))
        ldm = nb.now("LD")
        gdm = nb.now("GD")
        nb.wait("vector", "LD", ldm)
        nb.wait("vector", "GD", gdm)
        nb.wait("vector", "PE", nb.now("PE"))
        for g in range(8):
            nb.op("vector", "scalar_tensor_tensor", w1d[:, g, :],
                  fqb[par][:, g, :], -128.0, sc1b[par][:, g, :],
                  OP.add, OP.mult, inc="DV" if g == 7 else None)
        fq_free[par] = nb.now("DV")
        dv_w1d = nb.now("DV")
        if _DEBUG_SPILL and m == 0 and globals().get("_DUMP_H", False):
            for g in range(8):
                nb.store(sp["Y"][1024 + g * 128:1024 + (g + 1) * 128, 0:128],
                         w1d[:, g, :], waits=(("DV", dv_w1d),))
        for tau in range(NT):
            bi = tau % 2
            fw = (bank_free[bi], ("DV", dv_w1d))
            for g in range(8):
                nb.op("tensor", "matmul", pb[bi], w1d[:, g, :],
                      h_all[:, g, tau * TT:(tau + 1) * TT],
                      start=(g == 0), stop=(g == 7),
                      waits=fw if g == 0 else (),
                      inc="PE" if g == 7 else None)
            nb.wait("scalar", "PE", nb.now("PE"))
            nb.wait("scalar", "ST", s1st[bi])
            ac = nb.op("scalar", "activation", s1o[bi], pb[bi], AF.Silu,
                       bias=cst["fc1b"][:, m:m + 1], inc="AC")
            bank_free[bi] = ("AC", ac)
            s1st[bi] = nb.store(
                sp["Ssp"][m * 128:(m + 1) * 128, tau * TT:(tau + 1) * TT],
                s1o[bi], waits=(("AC", ac),))
        if _STOP_AFTER == f"fc1m{m}":
            break
    nb.barrier()
    if (_STOP_AFTER or "").startswith("fc1m"):
        _finish(nb)
        return nb, OUT

    o_f2d = 0
    o_fq2 = 16384
    o_sc2 = 17408
    o_sst = 18432
    o_oall = POOL_N - 16384
    f2d = P(o_f2d, 32, 512)
    fq2b = [P(o_fq2 + i * 512, 512, i32=True) for i in range(2)]
    sc2b = [P(o_sc2 + i * 512, 512) for i in range(2)]
    sstb = [P(o_sst + i * 512, 512) for i in range(2)]
    o_all = P(o_oall, 8, 2048)
    fq2_free = {0: 0, 1: 0}
    sst_free = {0: 0, 1: 0}
    for half in range(2):
        for m in range(32):
            par = m % 2
            nb.load(fq2b[par], fc2qv[:, m, half * 512:(half + 1) * 512],
                    waits=(("DV", fq2_free[par]),))
            nb.gdma(sc2b[par],
                    ins["fc2s"][m:m + 1, half * 512:(half + 1) * 512]
                    .broadcast_to([128, 512]),
                    waits=(("DV", fq2_free[par]),))
            nb.wait("vector", "LD", nb.now("LD"))
            nb.wait("vector", "GD", nb.now("GD"))
            fq2_free[par] = nb.op("vector", "scalar_tensor_tensor",
                                  f2d[:, m, :], fq2b[par], -128.0, sc2b[par],
                                  OP.add, OP.mult, inc="DV")
        dv_f2d = nb.now("DV")
        for tau in range(NT):
            for m in range(32):
                par = m % 2
                nb.load(sstb[par],
                        sp["Ssp"][m * 128:(m + 1) * 128, tau * TT:(tau + 1) * TT],
                        waits=(("PE", sst_free[par]),))
                nb.wait("tensor", "LD")
                if m == 0:
                    nb.wait("tensor", "DV", dv_f2d)
                    for n1 in range(4):
                        nb.wait("tensor", *bank_free[n1])
                for n1 in range(4):
                    nb.op("tensor", "matmul", pb[n1],
                          f2d[:, m, n1 * 128:(n1 + 1) * 128], sstb[par],
                          start=(m == 0), stop=(m == 31),
                          inc="PE" if n1 == 3 else None)
                sst_free[par] = nb.now("PE")
            nb.wait("scalar", "PE", nb.now("PE"))
            for n1 in range(4):
                nch = half * 4 + n1
                ac = nb.op("scalar", "activation",
                           o_all[:, nch, tau * TT:(tau + 1) * TT], pb[n1],
                           AF.Identity, scale=0.5, bias=fc2b05[:, nch:nch + 1],
                           inc="AC")
                bank_free[n1] = ("AC", ac)
    nb.barrier()
    if _DEBUG_SPILL:
        for k in range(8):
            nb.store(sp["Y"][k * 128:(k + 1) * 128, :], o_all[:, k, :])
        nb.barrier()

    x2l = P(0, 8, 512)
    sq4 = P(4096, 512)
    for tau in range(NT):
        nb.load(x2l, x2v[:, :, tau * TT:(tau + 1) * TT],
                waits=(("DV", nb.now("DV")), ("PE", nb.now("PE")),
                       ("AC", nb.now("AC"))))
        ld4 = nb.now("LD")
        nb.wait("vector", "LD", ld4)
        for k in range(8):
            nb.op("vector", "scalar_tensor_tensor", x2l[:, k, :], x2l[:, k, :],
                  2.0, o_all[:, k, tau * TT:(tau + 1) * TT], OP.mult, OP.add,
                  inc="DV" if k == 7 else None)
        stats_fm(x2l, sq4)
        for k in range(8):
            nb.op("vector", "tensor_sub", sq4, x2l[:, k, :], pb[6])
            nb.op("vector", "tensor_mul",
                  o_all[:, k, tau * TT:(tau + 1) * TT], sq4, pb[7], inc="DV")
            nb.store(outv[:, k, tau * TT:(tau + 1) * TT],
                     o_all[:, k, tau * TT:(tau + 1) * TT],
                     waits=(("DV", nb.now("DV")),))
        bank_free[6] = ("DV", nb.now("DV"))
        bank_free[7] = ("DV", nb.now("DV"))
    nb.barrier()
    _finish(nb)
    return nb, OUT


def _finish(nb):
    for s_ in ("LD", "ST", "GD", "CC"):
        if nb.now(s_):
            nb.wait("sync", s_, nb.now(s_))
    nb.check()


def emit_all(nb):
    nc = nb.nc
    import contextlib
    with contextlib.ExitStack() as ctx:
        for s_ in ("LD", "ST", "GD", "PE", "DV", "AC", "CC", "BAR"):
            nb.sems[s_] = ctx.enter_context(nc.semaphore(s_))
        block = ctx.enter_context(nc.Block())
        for eng in nb.ENGS:
            thunks = nb.prog[eng]
            getattr(block, eng)(lambda e, th=thunks: [t(e) for t in th])
    return nc


def kernel(**inputs):
    cores = _host_stage(inputs)
    nb, OUT = build_nc()
    nc = emit_all(nb)
    names = [n for n, _, _ in _IN_SPECS]
    in_maps = [{n: cores[c][n] for n in names} for c in range(NC)]
    res = run_bass_kernel_spmd(nc, in_maps, core_ids=list(range(NC)))
    out = np.zeros((B, S, DM), np.float32)
    for c in range(NC):
        b, q = c // 4, c % 4
        out[b, q * SC:(q + 1) * SC, :] = np.asarray(res.results[c]["OUT"]).T
    return out
